# version 06
# baseline (speedup 1.0000x reference)
import numpy as np
import ml_dtypes
from contextlib import ExitStack
from dataclasses import replace as dc_replace

import concourse.bass as bass
import concourse.mybir as mybir
from concourse.bass_utils import run_bass_kernel_spmd

F32 = mybir.dt.float32
BF16 = mybir.dt.bfloat16
I32 = mybir.dt.int32
AF = mybir.ActivationFunctionType
ALU = mybir.AluOpType
AX = mybir.AxisListType

NCORES = 8
D = 1024
H = 16
HD = 64
KM = D // 128
EPS = 1e-5
C_SHIFT = -8.0
DBG = False
ALIAS_ROPE = True
ALIAS_W = True


class Prog:

    def __init__(self):
        self.ops = []
        self.counts = {}
        self.history = {}

    def sem(self, name):
        self.counts.setdefault(name, 0)
        self.history.setdefault(name, [])

    def op(self, eng, fn, waits=(), inc=None, amt=1, dma=False):
        tok = None
        if inc is not None:
            self.sem(inc)
        if inc is not None:
            self.counts[inc] += amt
            self.history[inc].append((self.counts[inc], eng, dma))
            tok = (inc, self.counts[inc])
        ws = []
        for w in waits:
            if w is None:
                continue
            if isinstance(w, list):
                ws += [x for x in w if x is not None]
            else:
                ws.append(w)
        self.ops.append((eng, ws, (inc, amt) if inc else None, fn))
        return tok

    def dma(self, eng, fn, waits=(), inc="swr"):
        return self.op(eng, fn, waits, inc, amt=16, dma=True)

    def cur(self, sem):
        return (sem, self.counts[sem])

    def emit(self, engname, eng, sems):
        dma_sems = {s for s, hist in self.history.items()
                    if any(isdma for (_, _, isdma) in hist)}
        watermark = {}
        prefix = {s: 0 for s in self.counts}
        for (e, waits, inc, fn) in self.ops:
            if e == engname:
                merged = {}
                for s, v in waits:
                    if s in dma_sems:
                        v = prefix[s]
                    if v > merged.get(s, 0):
                        merged[s] = v
                for s, v in sorted(merged.items()):
                    if watermark.get(s, 0) >= v:
                        continue
                    eng.wait_ge(sems[s], v)
                    watermark[s] = v
                ins = fn(eng)
                if inc is not None:
                    ins.then_inc(sems[inc[0]], inc[1])
            if inc is not None:
                prefix[inc[0]] += inc[1]


class Slot:
    def __init__(self):
        self.users = []

    def take(self):
        u, self.users = self.users, []
        return u


def host_tables(S):
    half = HD // 2
    inv = 1.0 / (10000.0 ** (np.arange(0, HD, 2)[:half].astype(np.float64) / HD))
    ang = np.outer(np.arange(S, dtype=np.float64), inv)
    cos = np.cos(ang).astype(np.float32).T
    sin = np.sin(ang).astype(np.float32).T
    costab = np.empty((128, S), np.float32)
    sintab = np.empty((128, S), np.float32)
    for g in range(4):
        costab[g * 32:(g + 1) * 32] = cos
        sintab[g * 32:(g + 1) * 32] = sin * (1.0 if g % 2 == 0 else -1.0)
    masks = np.empty((128, 2048), np.float32)
    kk = np.arange(128)[:, None]
    qq = np.arange(512)[None, :]
    for j in range(4):
        masks[:, j * 512:(j + 1) * 512] = (qq >= kk + j * 128)
    bf = ml_dtypes.bfloat16
    return costab.astype(bf), sintab.astype(bf), masks.astype(bf)


def rope_perm():
    pm = []
    for h in range(H):
        base = h * HD
        pm += [base + i for i in range(0, HD, 2)]
        pm += [base + i for i in range(1, HD, 2)]
    return np.array(pm, np.int64)


def vt_head_ap(vt_t, head):
    full = vt_t[:, :]
    blockstep = (H * HD) - head * HD
    new_dims = [list(full.ap[0]), [blockstep, 2], [1, HD]]
    return dc_replace(full, offset=full.offset + head * HD, ap=new_dims)


def build_nc(SEQS, S):
    NT = S // 128
    NTG = NT // 4
    NQB = S // 512

    nc = bass.Bass("TRN2", target_bir_lowering=False, debug=False)

    x_in = nc.dram_tensor("x", [SEQS, S, D], F32, kind="ExternalInput").ap()
    wq_in = nc.dram_tensor("wq", [D, D], F32, kind="ExternalInput").ap()
    wk_in = nc.dram_tensor("wk", [D, D], F32, kind="ExternalInput").ap()
    wv_in = nc.dram_tensor("wv", [D, D], F32, kind="ExternalInput").ap()
    wo_in = nc.dram_tensor("wo", [D, D], F32, kind="ExternalInput").ap()
    wob_in = nc.dram_tensor("wob", [1, D], F32, kind="ExternalInput").ap()
    fw_in = nc.dram_tensor("fw", [D, D], F32, kind="ExternalInput").ap()
    fb_in = nc.dram_tensor("fb", [1, D], F32, kind="ExternalInput").ap()
    id32_in = nc.dram_tensor("id32", [128, 128], F32, kind="ExternalInput").ap()
    idbf_in = nc.dram_tensor("idbf", [128, 128], BF16, kind="ExternalInput").ap()
    cos_in = nc.dram_tensor("costab", [128, S], BF16, kind="ExternalInput").ap()
    sin_in = nc.dram_tensor("sintab", [128, S], BF16, kind="ExternalInput").ap()
    mask_in = nc.dram_tensor("masks", [128, 2048], BF16, kind="ExternalInput").ap()
    ccol_in = nc.dram_tensor("ccol_c", [128, 16], F32, kind="ExternalInput").ap()
    crow_in = nc.dram_tensor("crow_c", [1, 16], F32, kind="ExternalInput").ap()
    onesf_in = nc.dram_tensor("onesf_c", [1, 128], F32, kind="ExternalInput").ap()
    onesb_in = nc.dram_tensor("onesb_c", [1, 128], BF16, kind="ExternalInput").ap()
    vtones_in = nc.dram_tensor("vtones_c", [128, 64], BF16, kind="ExternalInput").ap()
    out = nc.dram_tensor("out", [SEQS, S, D], F32, kind="ExternalOutput").ap()

    xp_scr = nc.dram_tensor("xp_scr", [SEQS * S, D], F32).ap()
    xpp_scr = nc.dram_tensor("xpp_scr", [SEQS * S, D], F32).ap()
    wscr = {w: nc.dram_tensor(f"{w}_scr", [D, D], BF16).ap()
            for w in ["wq", "wk", "wv", "wo", "qwT"]}
    cci = [nc.dram_tensor(f"cc{i}_in", [1, 16], F32) for i in range(2)]
    cco = [nc.dram_tensor(f"cc{i}_out", [1, 16], F32) for i in range(2)]

    es = ExitStack()
    sbuf = lambda n, sh, dt: es.enter_context(nc.sbuf_tensor(n, sh, dt))
    psumt = lambda n, sh, dt: es.enter_context(nc.psum_tensor(n, sh, dt))

    id32 = sbuf("c_id32", [128, 128], F32)
    idbf = sbuf("c_idbf", [128, 128], BF16)
    costab = sbuf("c_cos", [128, S], BF16)
    sintab = sbuf("c_sin", [128, S], BF16)
    masks = sbuf("c_masks", [128, 2048], BF16)
    onesf = sbuf("c_onesf", [1, 128], F32)
    onesb = sbuf("c_onesb", [1, 128], BF16)
    wob_row = sbuf("c_wobrow", [1, D], BF16)
    fb_row = sbuf("c_fbrow", [1, D], BF16)
    brow = sbuf("c_brow", [1, D], BF16)

    NXR = 4
    xr = [sbuf(f"b_xr{i}", [128, D], F32) for i in range(NXR)]
    xo = [sbuf(f"b_xo{i}", [128, D], F32) for i in range(2)]
    yb0 = sbuf("b_y0", [128, D], F32)
    yb = [yb0, yb0]
    ib0 = sbuf("b_i0", [128, D], I32)
    ib = [ib0, ib0]
    qab0 = sbuf("b_qa0", [128, D], BF16)
    qab = [qab0, qab0]
    c1b = sbuf("b_c1", [128, D], F32)
    c2b = sbuf("b_c2", [128, D], F32)
    qbf0 = sbuf("b_qbf0", [128, D], BF16)
    qbf = [qbf0, qbf0]
    qfm = [sbuf(f"b_qfm{i}", [128, D], BF16) for i in range(4)]

    st1m = sbuf("s_st1m", [128, SEQS * NT], F32)
    st1r = sbuf("s_st1r", [128, SEQS * NT], F32)
    st3m = sbuf("s_st3m", [128, SEQS * NT], F32)
    st3r = sbuf("s_st3r", [128, SEQS * NT], F32)
    ricol = sbuf("s_ricol", [128, 2], F32)
    alcol = sbuf("s_alcol", [128, 2], F32)
    thrcol = sbuf("s_thrcol", [128, 1], F32)
    acol = sbuf("s_acol", [128, 2], F32)
    scal = sbuf("s_scal", [1, 16], F32)
    ccres = sbuf("s_ccres", [1, 32], F32)
    stt = sbuf("s_stt", [128, 8], F32)
    sqs = sbuf("s_sqs", [128, D], BF16)
    z2tm = sqs
    maxacc = sbuf("s_maxacc", [128, 2], F32)
    cshift = sbuf("s_cshift", [128, 1], F32)
    chalf = sbuf("s_chalf", [128, 1], F32)
    ccol = sbuf("s_ccol", [128, 16], F32)
    crow = sbuf("s_crow", [1, 16], F32)
    mx128 = sbuf("s_mx128", [1, 128], F32)

    z2 = sbuf("a_z2", [128, KM * S], BF16)
    kfm = [sbuf(f"a_kfm{i}", [128, S], BF16) for i in range(2)]
    vpair = [sbuf(f"a_vp{i}", [128, S], BF16) for i in range(2)]
    ones64 = sbuf("a_ones64", [128, HD], BF16)
    attn = sbuf("a_attn", [128, KM * S], BF16)
    ktmp = sbuf("a_ktmp", [128, S], BF16)
    qhp = [sbuf(f"a_qhp{i}", [128, S], BF16) for i in range(2)]
    ropet = sbuf("a_ropet", [128, 512], BF16)
    ropes = sbuf("a_ropes", [128, 512], BF16)
    kropet = sbuf("a_kropet", [128, 512], BF16)
    kropes = sbuf("a_kropes", [128, 512], BF16)
    ptile = [sbuf(f"a_pt{i}", [128, 512], BF16) for i in range(3)]
    recb2 = sbuf("a_rec", [128, 512], F32)
    recb = [recb2[0:64, :], recb2[64:128, :]]

    wr = [sbuf(f"w_r{i}", [128, D], BF16) for i in range(2)]
    if ALIAS_W:
        wstage = xo
        absw = c1b
    else:
        wstage = [sbuf(f"w_st{i}", [128, D], F32) for i in range(2)]
        absw = sbuf("w_absw", [128, D], F32)
    wcast = sbuf("w_cast", [128, D], BF16)
    sgnw = qab0
    gtw = sbuf("w_gtw", [128, D], BF16)
    qwc = sbuf("w_qwc", [128, D], BF16)
    qwt_ev = qbf[0] if ALIAS_W else sbuf("w_qwt", [128, D], BF16)

    bank = [psumt(f"p_b{i}", [128, 512], F32) for i in range(8)]

    p = Prog()
    for s_ in ["ld", "ts", "vs", "ss", "gs", "cc", "sout", "swr", "gld"]:
        p.sem(s_)

    last_macc = [None, None]
    prev_tqf = [None]
    krope_prev = [None]
    sl_xr = [Slot() for _ in range(NXR)]
    sl_xo = [Slot() for _ in range(2)]
    _sl_y0 = Slot()
    sl_y = [_sl_y0, _sl_y0]
    _sl_i0 = Slot()
    sl_i = [_sl_i0, _sl_i0]
    _sl_qa0 = Slot()
    sl_qa = [_sl_qa0, _sl_qa0]
    _sl_qbf0 = Slot()
    sl_qbf = [_sl_qbf0, _sl_qbf0]
    sl_qfm = [Slot() for _ in range(4)]
    sl_wr = [Slot() for _ in range(2)]
    sl_wst = [Slot() for _ in range(2)]
    sl_wcast = Slot()
    sl_bank = [Slot() for _ in range(8)]
    sl_pt = [Slot() for _ in range(3)]
    sl_rec = [Slot() for _ in range(2)]
    sl_qhp = [Slot() for _ in range(2)]
    sl_c1 = Slot()
    sl_c2 = Slot()
    sl_u1 = Slot()
    sl_u2 = Slot()
    sl_r1 = Slot()
    sl_r2 = Slot()
    sl_absw = Slot()
    sl_qwc = Slot()
    sl_qwtev = Slot()
    sl_rope = Slot()
    sl_sttq = Slot()
    sl_stt4 = Slot()
    sl_z2 = Slot()
    sl_kfm = [Slot() for _ in range(2)]
    sl_attn = Slot()
    sl_ktmp = Slot()
    sl_vp = [Slot() for _ in range(2)]
    sl_z2tm = Slot()
    wscr_tok = {k: {} for k in wscr}
    xp_w = {}
    xpp_w = {}

    def vsm(fn, waits):
        t_ = p.op("vector", fn, waits, inc="vs")
        return p.op("vector", lambda e: e.drain(), [t_], inc="vs")

    def ssm(fn, waits):
        t_ = p.op("scalar", fn, waits, inc="ss")
        return p.op("scalar", lambda e: e.drain(), [t_], inc="ss")

    def bcast_col(src11, dstcol, waits):
        bw = sl_bank[7].take()
        t1 = p.op("tensor", lambda e: e.matmul(bank[7][:, 0:1], onesf[:, :], src11,
                                               start=True, stop=True),
                  waits=list(waits) + bw, inc="ts")
        t2 = vsm(lambda e: e.tensor_copy(dstcol, bank[7][:, 0:1]), [t1])
        sl_bank[7].users.append(t2)
        return t2

    def part_reduce_max(srccol, dst11, waits):
        bw = sl_bank[7].take()
        t1 = p.op("tensor", lambda e: e.transpose(bank[7][0:1, 0:128], srccol, id32[:, :]),
                  waits=list(waits) + bw, inc="ts")
        t2 = vsm(lambda e: e.tensor_copy(mx128[:, :], bank[7][0:1, 0:128]), [t1])
        sl_bank[7].users.append(t2)
        t3 = vsm(lambda e: e.tensor_reduce(dst11, mx128[:, :], AX.X, ALU.max), [t2])
        return t3

    def ln_stats(xap, mcol_ap, rcol_ap, waits_x):
        tv0 = vsm(lambda e: e.tensor_reduce(stt[:, 0:1], xap, AX.X, ALU.add), waits_x)
        ts0_ = p.op("scalar", lambda e: e.activation(sqs[:, :], xap, AF.Square,
                                                     accum_out=stt[:, 1:2]),
                    waits=list(waits_x) + sl_sttq.take() + sl_z2tm.take(), inc="ss")
        ts0 = p.op("scalar", lambda e: e.drain(), [ts0_], inc="ss")
        t1 = vsm(lambda e: e.tensor_tensor(
            mcol_ap, stt[:, 0:1], ccol[:, 0:1], ALU.mult), [tv0] + t_cc0)
        t2 = vsm(lambda e: e.tensor_tensor(
            stt[:, 2:3], mcol_ap, mcol_ap, ALU.mult), [t1])
        t3 = vsm(lambda e: e.tensor_tensor(
            stt[:, 3:4], stt[:, 1:2], ccol[:, 0:1], ALU.mult), [ts0])
        sl_sttq.users.append(t3)
        t4 = vsm(lambda e: e.tensor_tensor(
            stt[:, 3:4], stt[:, 3:4], stt[:, 2:3], ALU.subtract), [t2, t3])
        t5 = vsm(lambda e: e.tensor_tensor(
            stt[:, 3:4], stt[:, 3:4], ccol[:, 1:2], ALU.add), [t4])
        t6a = ssm(lambda e: e.activation(stt[:, 5:6], stt[:, 3:4], AF.Sqrt),
                  [t5] + sl_stt4.take())
        t6 = vsm(lambda e: e.reciprocal(stt[:, 4:5], stt[:, 5:6]), [t6a])
        t7 = vsm(lambda e: e.tensor_tensor(
            stt[:, 5:6], stt[:, 4:5], stt[:, 4:5], ALU.mult), [t6])
        t8 = vsm(lambda e: e.tensor_tensor(
            stt[:, 5:6], stt[:, 5:6], stt[:, 3:4], ALU.mult), [t7])
        t9a = vsm(lambda e: e.tensor_tensor(
            stt[:, 5:6], stt[:, 5:6], ccol[:, 2:3], ALU.subtract), [t8])
        t9 = vsm(lambda e: e.tensor_tensor(
            stt[:, 5:6], stt[:, 5:6], ccol[:, 3:4], ALU.mult), [t9a])
        t10 = vsm(lambda e: e.tensor_tensor(
            rcol_ap, stt[:, 5:6], stt[:, 4:5], ALU.mult), [t9])
        sl_stt4.users.append(t10)
        return t10, ts0

    def newton_recip(src11, dst11, waits):
        t1 = vsm(lambda e: e.reciprocal(dst11, src11), waits)
        t2 = vsm(lambda e: e.tensor_tensor(scal[0:1, 14:15], dst11, src11, ALU.mult), [t1])
        t3 = vsm(lambda e: e.tensor_tensor(
            scal[0:1, 14:15], crow[0:1, 0:1], scal[0:1, 14:15], ALU.subtract),
            [t2] + t_cc0)
        t4 = vsm(lambda e: e.tensor_tensor(dst11, dst11, scal[0:1, 14:15], ALU.mult), [t3])
        return t4

    def transpose_tile(src_ap, dst_ap, waits, strided_dst=False, bank_i=0):
        bw = sl_bank[bank_i].take()
        last = None
        for j in range(KM):
            last = p.op("tensor", lambda e, j=j: e.transpose(
                bank[bank_i].bitcast(BF16)[:, j * 128:(j + 1) * 128],
                src_ap[:, j * 128:(j + 1) * 128], idbf[:, :]),
                waits=(list(waits) + bw + [t_idbf]) if j == 0 else [], inc="ts")
        if strided_dst:
            src_ps = bank[bank_i].bitcast(BF16)[:, 0:1024].rearrange("p (j c) -> p j c", j=KM)
        else:
            src_ps = bank[bank_i].bitcast(BF16)[:, 0:1024]
        tev = p.op("vector", lambda e: e.tensor_copy(dst_ap, src_ps),
                   waits=[last], inc="vs")
        sl_bank[bank_i].users.append(tev)
        return tev, last

    def quant_chain(src_fn, m_ap, r_ap, ri_ap, src_users, extra_waits, sy):
        ta_ = p.op("vector", lambda e: e.tensor_tensor(
            acol[:, 0:1], r_ap, ri_ap, ALU.mult), waits=extra_waits, inc="vs")
        ta = p.op("vector", lambda e: e.drain(), waits=[ta_], inc="vs")
        ty = p.op("vector", lambda e: e.tensor_scalar(
            yb[sy][:, :], src_fn(), m_ap, acol[:, 0:1],
            ALU.subtract, ALU.mult), waits=[ta] + sl_y[sy].take(), inc="vs")
        src_users.append(ty)
        ti = p.op("vector", lambda e: e.tensor_copy(ib[sy][:, :], yb[sy][:, :]),
                  waits=[ti_w for ti_w in [ty]] + sl_i[sy].take(), inc="vs")
        tqa = p.op("vector", lambda e: e.tensor_copy(qab[sy][:, :], ib[sy][:, :]),
                   waits=[ti] + sl_qa[sy].take(), inc="vs")
        td_ = p.op("gpsimd", lambda e: e.tensor_tensor(
            c2b[:, :], yb[sy][:, :], ib[sy][:, :], ALU.subtract),
            waits=[ti] + sl_c2.take(), inc="gs")
        sl_y[sy].users.append(td_)
        sl_i[sy].users.append(td_)
        tu1 = p.op("scalar", lambda e: e.activation(
            gtw[:, :], c2b[:, :], AF.Sign, bias=chalf[:, 0:1], scale=1.0),
            waits=[td_, t_chf] + sl_u1.take(), inc="ss")
        tr1 = p.op("scalar", lambda e: e.activation(qwc[:, :], gtw[:, :], AF.Relu),
                   waits=[tu1] + sl_r1.take(), inc="ss")
        sl_u1.users.append(tr1)
        tu2 = p.op("scalar", lambda e: e.activation(
            gtw[:, :], c2b[:, :], AF.Sign, bias=chalf[:, 0:1], scale=-1.0),
            waits=[td_, tr1], inc="ss")
        sl_c2.users += [tu1, tu2]
        tr2 = p.op("scalar", lambda e: e.activation(wcast[:, :], gtw[:, :], AF.Relu),
                   waits=[tu2] + sl_r2.take(), inc="ss")
        sl_u1.users.append(tr2)
        te = p.op("gpsimd", lambda e: e.tensor_tensor(
            c1b[:, :], qwc[:, :], wcast[:, :], ALU.subtract),
            waits=[tr1, tr2] + sl_c1.take(), inc="gs")
        sl_r1.users.append(te)
        sl_r2.users.append(te)
        tq = p.op("vector", lambda e: e.tensor_tensor(
            qbf[sy][:, :], qab[sy][:, :], c1b[:, :], ALU.add),
            waits=[te, tqa] + sl_qbf[sy].take(), inc="vs")
        sl_qa[sy].users.append(tq)
        sl_c1.users.append(tq)
        return tq

    def ffn_block(seq, ttg, src_of_tt, m_of, r_of, ri_ap, al_ap, brow_tok,
                  ri_tok, al_tok, evac_fn):
        ttoks = []
        for t4 in range(4):
            tt = ttg * 4 + t4
            idx = seq * NT + tt
            slot = idx % NXR
            src, srcw = src_of_tt(tt)
            tl = p.dma("sync", lambda e, src=src, slot=slot: e.dma_start(
                out=xr[slot][:, :], in_=src),
                waits=sl_xr[slot].take() + ([srcw] if srcw else []), inc=f"xin{slot}")
            tq = quant_chain(lambda slot=slot: xr[slot][:, :],
                             m_of(idx), r_of(idx), ri_ap,
                             sl_xr[slot].users, [tl, ri_tok], idx % 2)
            tfm, _ = transpose_tile(qbf[idx % 2][:, :], qfm[t4][:, :],
                                    [tq] + sl_qfm[t4].take())
            sl_qbf[idx % 2].users.append(tfm)
            ttoks.append((slot, tfm))
        mm_last = [None] * 8
        for km in range(KM):
            sw = km % 2
            twl = p.dma("sync", lambda e, km=km, sw=sw: e.dma_start(
                out=wr[sw][:, :], in_=wscr["qwT"][km * 128:(km + 1) * 128, :]),
                waits=sl_wr[sw].take() + [wscr_tok["qwT"][km]], inc=f"wld{sw}")
            for t4 in range(4):
                for oh in range(2):
                    b = t4 * 2 + oh
                    bw = sl_bank[b].take() if km == 0 else []
                    mm_last[b] = p.op("tensor", lambda e, km=km, sw=sw, t4=t4, oh=oh, b=b: e.matmul(
                        bank[b][:, :], qfm[t4][:, km * 128:(km + 1) * 128],
                        wr[sw][:, oh * 512:(oh + 1) * 512],
                        start=(km == 0), stop=False),
                        waits=[twl, ttoks[t4][1]] + bw, inc="ts")
                    sl_wr[sw].users.append(mm_last[b])
                    sl_qfm[t4].users.append(mm_last[b])
        for t4 in range(4):
            for oh in range(2):
                b = t4 * 2 + oh
                mm_last[b] = p.op("tensor", lambda e, oh=oh, b=b: e.matmul(
                    bank[b][:, :], onesb[:, :], brow[0:1, oh * 512:(oh + 1) * 512],
                    start=False, stop=True),
                    waits=[mm_last[b], brow_tok, t_onesb], inc="ts")
        for t4 in range(4):
            tt = ttg * 4 + t4
            evac_fn(t4, tt, ttoks[t4][0], mm_last, al_ap, al_tok)

    t_id32 = p.dma("sync", lambda e: e.dma_start(out=id32[:, :], in_=id32_in[:, :]), inc="ld")
    t_idbf = p.dma("sync", lambda e: e.dma_start(out=idbf[:, :], in_=idbf_in[:, :]), inc="ld")
    t_cos = p.dma("sync", lambda e: e.dma_start(out=costab[:, :], in_=cos_in[:, :]), inc="ld")
    t_sin = p.dma("sync", lambda e: e.dma_start(out=sintab[:, :], in_=sin_in[:, :]), inc="ld")
    t_mask = p.dma("sync", lambda e: e.dma_start(out=masks[:, :], in_=mask_in[:, :]), inc="ld")
    t_fb = p.dma("gpsimd", lambda e: e.dma_start(out=fb_row[:, :], in_=fb_in[:, :]), inc="gld")
    t_wob0 = p.dma("sync", lambda e: e.dma_start(out=wstage[0][0:1, :], in_=wob_in[:, :]), inc="wst0")
    t_ones = p.dma("sync", lambda e: e.dma_start(out=onesf[:, :], in_=onesf_in[:, :]), inc="ld")
    t_onesb = p.dma("sync", lambda e: e.dma_start(out=onesb[:, :], in_=onesb_in[:, :]), inc="ld")
    t_ccl = p.dma("sync", lambda e: e.dma_start(out=ccol[:, :], in_=ccol_in[:, :]), inc="ld")
    t_crl = p.dma("sync", lambda e: e.dma_start(out=crow[:, :], in_=crow_in[:, :]), inc="ld")
    t_mz = p.dma("sync", lambda e: e.dma_start(out=maxacc[:, :], in_=ccol_in[:, 6:8], allow_slow_non_contiguous=True), inc="ld")
    t_csh = p.dma("sync", lambda e: e.dma_start(out=cshift[:, :], in_=ccol_in[:, 4:5], allow_slow_non_contiguous=True), inc="ld")
    t_chf = p.dma("sync", lambda e: e.dma_start(out=chalf[:, :], in_=ccol_in[:, 5:6], allow_slow_non_contiguous=True), inc="ld")
    t_cc0 = [t_ccl, t_crl]
    t_wob = p.op("vector", lambda e: e.tensor_copy(wob_row[:, :], wstage[0][0:1, :]),
                 waits=[t_wob0], inc="vs")
    sl_wst[0].users.append(t_wob)
    t_vones = p.dma("sync", lambda e: e.dma_start(out=ones64[:, :], in_=vtones_in[:, :]),
                    inc="ld")

    for wname, wap in [("wq", wq_in), ("wk", wk_in), ("wv", wv_in), ("wo", wo_in)]:
        for km in range(KM):
            s = km % 2
            tl = p.dma("sync", lambda e, wap=wap, km=km, s=s: e.dma_start(
                out=wstage[s][:, :], in_=wap[km * 128:(km + 1) * 128, :]),
                waits=sl_wst[s].take(), inc=f"wst{s}")
            tc = p.op("vector", lambda e, s=s: e.tensor_copy(wcast[:, :], wstage[s][:, :]),
                      waits=[tl] + sl_wcast.take(), inc="vs")
            sl_wst[s].users.append(tc)
            td = p.dma("gpsimd", lambda e, wname=wname, km=km: e.dma_start(
                out=wscr[wname][km * 128:(km + 1) * 128, :], in_=wcast[:, :]),
                waits=[tc], inc="swr")
            sl_wcast.users.append(td)
            wscr_tok[wname][km] = td

    prev_add = None
    for km in range(KM):
        s = km % 2
        tl = p.dma("sync", lambda e, km=km, s=s: e.dma_start(
            out=wstage[s][:, :], in_=fw_in[km * 128:(km + 1) * 128, :]),
            waits=sl_wst[s].take(), inc=f"wst{s}")
        col = 6 if km == 0 else 7
        ta = p.op("scalar", lambda e, s=s, col=col: e.activation(
            absw[:, :], wstage[s][:, :], AF.Abs, accum_out=stt[:, col:col + 1]),
            waits=[tl] + ([prev_add] if km > 0 else []), inc="ss")
        sl_wst[s].users.append(ta)
        if km == 0:
            prev_add = p.op("scalar", lambda e: e.drain(), [ta], inc="ss")
        else:
            tadr = p.op("scalar", lambda e: e.drain(), [ta], inc="ss")
            prev_add = vsm(lambda e: e.tensor_tensor(
                stt[:, 6:7], stt[:, 6:7], stt[:, 7:8], ALU.add), [tadr])
    t_ws_tr = p.op("tensor", lambda e: e.transpose(bank[7][0:1, 0:128], stt[:, 6:7], id32[:, :]),
                   waits=[prev_add, t_id32] + sl_bank[7].take(), inc="ts")
    t_ws_cp = vsm(lambda e: e.tensor_copy(mx128[:, :], bank[7][0:1, 0:128]), [t_ws_tr])
    sl_bank[7].users.append(t_ws_cp)
    t_ws = vsm(lambda e: e.tensor_reduce(scal[0:1, 0:1], mx128[:, :], AX.X, ALU.add), [t_ws_cp])
    t_ws2 = vsm(lambda e: e.tensor_tensor(
        scal[0:1, 0:1], scal[0:1, 0:1], crow[0:1, 1:2], ALU.mult), [t_ws] + t_cc0)
    t_thr = vsm(lambda e: e.tensor_tensor(
        scal[0:1, 1:2], scal[0:1, 0:1], crow[0:1, 2:3], ALU.mult), [t_ws2])
    t_thrb = bcast_col(scal[0:1, 1:2], thrcol[:, 0:1], [t_thr])

    for km in range(KM):
        s = km % 2
        tl = p.dma("sync", lambda e, km=km, s=s: e.dma_start(
            out=wstage[s][:, :], in_=fw_in[km * 128:(km + 1) * 128, :]),
            waits=sl_wst[s].take(), inc=f"wst{s}")
        tabs = p.op("scalar", lambda e, s=s: e.activation(absw[:, :], wstage[s][:, :], AF.Abs),
                    waits=[tl] + sl_absw.take(), inc="ss")
        tsg = p.op("scalar", lambda e, s=s: e.activation(sgnw[:, :], wstage[s][:, :], AF.Sign),
                   waits=[tl] + sl_u2.take(), inc="ss")
        sl_wst[s].users += [tabs, tsg]
        tgt = p.op("vector", lambda e: e.tensor_scalar(
            gtw[:, :], absw[:, :], thrcol[:, 0:1], None, ALU.is_gt),
            waits=[tabs, t_thrb] + sl_u1.take(), inc="vs")
        sl_absw.users.append(tgt)
        tqw = p.op("vector", lambda e: e.tensor_tensor(qwc[:, :], sgnw[:, :], gtw[:, :], ALU.mult),
                   waits=[tgt, tsg] + sl_qwc.take(), inc="vs")
        sl_u1.users.append(tqw)
        sl_u2.users.append(tqw)
        bw = sl_bank[0].take()
        last = None
        for j in range(KM):
            last = p.op("tensor", lambda e, j=j: e.transpose(
                bank[0].bitcast(BF16)[:, j * 128:(j + 1) * 128],
                qwc[:, j * 128:(j + 1) * 128], idbf[:, :]),
                waits=([tqw, t_idbf] + bw) if j == 0 else [], inc="ts")
        sl_qwc.users.append(last)
        tev = p.op("vector", lambda e: e.tensor_copy(qwt_ev[:, :], bank[0].bitcast(BF16)[:, 0:1024]),
                   waits=[last] + sl_qwtev.take(), inc="vs")
        sl_bank[0].users.append(tev)
        dst = wscr["qwT"].rearrange("(j r) c -> r j c", r=128)[:, :, km * 128:(km + 1) * 128]
        src = qwt_ev[:, :].rearrange("p (j c) -> p j c", j=KM)
        tqd = p.dma("gpsimd", lambda e, dst=dst, src=src: e.dma_start(out=dst, in_=src),
                    waits=[tev], inc="swr")
        sl_bank[0].users.append(tqd)
        sl_qwtev.users.append(tqd)
        wscr_tok["qwT"][km] = tqd
        sl_r1.users.append(last)
        sl_r2.users.append(sl_wcast.users[-1] if sl_wcast.users else None)

    sl_qa[0].users += sl_u2.take()
    sl_c1.users += sl_absw.take()
    sl_qbf[0].users += sl_qwtev.take()
    for i_ in range(2):
        sl_xo[i_].users += sl_wst[i_].take()

    for seq in range(SEQS):
        for tt in range(NT):
            idx = seq * NT + tt
            slot = idx % NXR
            tl = p.dma("sync", lambda e, seq=seq, tt=tt, slot=slot: e.dma_start(
                out=xr[slot][:, :], in_=x_in[seq, tt * 128:(tt + 1) * 128, :]),
                waits=sl_xr[slot].take(), inc=f"xin{slot}")
            tv, tsc = ln_stats(xr[slot][:, :], st1m[:, idx:idx + 1], st1r[:, idx:idx + 1], [tl])
            tz = p.op("vector", lambda e, slot=slot, idx=idx: e.tensor_scalar(
                yb[0][:, :], xr[slot][:, :], st1m[:, idx:idx + 1], st1r[:, idx:idx + 1],
                ALU.subtract, ALU.mult), waits=[tv] + sl_y[0].take(), inc="vs")
            sl_xr[slot].users += [tz, tsc]
            tm = vsm(lambda e: e.tensor_reduce(
                stt[:, 0:1], yb[0][:, :], AX.X, ALU.max, apply_absolute_value=True), [tz])
            sl_y[0].users.append(tm)
            last_macc[0] = vsm(lambda e: e.tensor_tensor(
                maxacc[:, 0:1], maxacc[:, 0:1], stt[:, 0:1], ALU.max),
                [tm, t_mz] + ([last_macc[0]] if last_macc[0] else []))

    def allreduce_scale(maxcol, cc_idx, ri_dst, al_dst):
        t_drain = p.op("vector", lambda e: e.drain(), waits=[p.cur("vs")], inc="vs")
        t_pm = part_reduce_max(maxcol, scal[0:1, 2:3], [t_drain])
        t_bc = vsm(lambda e: e.tensor_copy(
            scal[0:1, 3:16], scal[0:1, 2:3].broadcast_to((1, 13))), [t_pm])
        t_ccw = p.dma("gpsimd", lambda e: e.dma_start(
            out=cci[cc_idx].ap()[:, :], in_=scal[0:1, 0:16]), waits=[t_bc, t_ws2], inc="swr")
        t_cc = p.op("gpsimd", lambda e: e.collective_compute(
            "AllReduce", ALU.max, replica_groups=[list(range(NCORES))],
            ins=[cci[cc_idx].ap().opt()], outs=[cco[cc_idx].ap().opt()]),
            waits=[t_ccw], inc="cc", dma=True)
        t_ccr = p.dma("gpsimd", lambda e: e.dma_start(
            out=ccres[0:1, cc_idx * 16:(cc_idx + 1) * 16], in_=cco[cc_idx].ap()[0:1, 0:16]),
            waits=[t_cc], inc="gld")
        t_is = vsm(lambda e: e.tensor_tensor(
            scal[0:1, 4:5], ccres[0:1, cc_idx * 16 + 2:cc_idx * 16 + 3],
            crow[0:1, 3:4], ALU.mult),
            [t_ccr] + t_cc0)
        t_ri = newton_recip(scal[0:1, 4:5], scal[0:1, 5:6], [t_is])
        t_al = vsm(lambda e: e.tensor_tensor(
            scal[0:1, 6:7], scal[0:1, 0:1], scal[0:1, 4:5], ALU.mult), [t_is])
        t_alb = vsm(lambda e: e.tensor_tensor(
            scal[0:1, 6:7], scal[0:1, 6:7], crow[0:1, 2:3], ALU.mult), [t_al])
        t_wsis = vsm(lambda e: e.tensor_tensor(
            scal[0:1, 7:8], scal[0:1, 0:1], scal[0:1, 4:5], ALU.mult), [t_is])
        t_rb = newton_recip(scal[0:1, 7:8], scal[0:1, 8:9], [t_wsis])
        t_brow_ = p.op("vector", lambda e: e.tensor_scalar(
            brow[:, :], fb_row[:, :], scal[0:1, 8:9], None, ALU.mult),
            waits=[t_rb, t_fb], inc="vs")
        t_brow = p.op("vector", lambda e: e.drain(), [t_brow_], inc="vs")
        t_ric = bcast_col(scal[0:1, 5:6], ri_dst, [t_ri])
        t_alc = bcast_col(scal[0:1, 6:7], al_dst, [t_alb])
        return t_ric, t_alc, t_brow

    t_ri1c, t_al1c, t_brow1 = allreduce_scale(maxacc[:, 0:1], 0, ricol[:, 0:1], alcol[:, 0:1])

    for seq in range(SEQS):
        z2toks = []

        def ffn1_evac(t4, tt, slot, mm_last, al_ap, al_tok, seq=seq):
            idx = seq * NT + tt
            so = tt % 2
            tw_xo = sl_xo[so].take()
            tev = []
            for oh in range(2):
                b = t4 * 2 + oh
                tev.append(p.op("vector", lambda e, b=b, oh=oh, slot=slot, so=so, al_ap=al_ap: e.scalar_tensor_tensor(
                    xo[so][:, oh * 512:(oh + 1) * 512], bank[b][:, :], al_ap,
                    xr[slot][:, oh * 512:(oh + 1) * 512], ALU.mult, ALU.add),
                    waits=[mm_last[b], al_tok] + (tw_xo if oh == 0 else []), inc="vs"))
                sl_bank[b].users.append(tev[-1])
            sl_xr[slot].users += tev
            tv0 = vsm(lambda e, so=so: e.tensor_reduce(
                stt[:, 0:1], xo[so][:, :], AX.X, ALU.add), tev)
            ts0_ = p.op("scalar", lambda e, so=so: e.activation(
                sqs[:, :], xo[so][:, :], AF.Square, accum_out=stt[:, 1:2]),
                waits=tev + sl_sttq.take() + sl_z2tm.take(), inc="ss")
            ts0 = p.op("scalar", lambda e: e.drain(), [ts0_], inc="ss")
            t1 = vsm(lambda e: e.tensor_tensor(
                acol[:, 1:2], stt[:, 0:1], ccol[:, 0:1], ALU.mult), [tv0] + t_cc0)
            t2 = vsm(lambda e: e.tensor_tensor(
                stt[:, 2:3], acol[:, 1:2], acol[:, 1:2], ALU.mult), [t1])
            t3 = vsm(lambda e: e.tensor_tensor(
                stt[:, 3:4], stt[:, 1:2], ccol[:, 0:1], ALU.mult), [ts0])
            sl_sttq.users.append(t3)
            t4_ = vsm(lambda e: e.tensor_tensor(
                stt[:, 3:4], stt[:, 3:4], stt[:, 2:3], ALU.subtract), [t2, t3])
            t5 = vsm(lambda e: e.tensor_tensor(
                stt[:, 3:4], stt[:, 3:4], ccol[:, 1:2], ALU.add), [t4_])
            t6a = ssm(lambda e: e.activation(stt[:, 5:6], stt[:, 3:4], AF.Sqrt),
                      [t5] + sl_stt4.take())
            t6 = vsm(lambda e: e.reciprocal(stt[:, 4:5], stt[:, 5:6]), [t6a])
            t7 = vsm(lambda e: e.tensor_tensor(
                stt[:, 5:6], stt[:, 4:5], stt[:, 4:5], ALU.mult), [t6])
            t8 = vsm(lambda e: e.tensor_tensor(
                stt[:, 5:6], stt[:, 5:6], stt[:, 3:4], ALU.mult), [t7])
            t9a = vsm(lambda e: e.tensor_tensor(
                stt[:, 5:6], stt[:, 5:6], ccol[:, 2:3], ALU.subtract), [t8])
            t9 = vsm(lambda e: e.tensor_tensor(
                stt[:, 5:6], stt[:, 5:6], ccol[:, 3:4], ALU.mult), [t9a])
            t10 = vsm(lambda e: e.tensor_tensor(
                stt[:, 6:7], stt[:, 5:6], stt[:, 4:5], ALU.mult), [t9])
            sl_stt4.users.append(t10)
            tz = p.op("vector", lambda e, so=so: e.tensor_scalar(
                z2tm[:, :], xo[so][:, :], acol[:, 1:2], stt[:, 6:7],
                ALU.subtract, ALU.mult), waits=[t10, ts0] + sl_z2tm.take(), inc="vs")
            dstz = z2[:, :].rearrange("p (k t) -> p k t", k=KM)[:, :, tt * 128:(tt + 1) * 128]
            tfm, tlast_tr = transpose_tile(z2tm[:, :], dstz,
                                           [tz] + (sl_z2.take() if tt == 0 else []),
                                           strided_dst=True)
            sl_z2tm.users += [tfm, tlast_tr]
            td = p.dma("gpsimd", lambda e, seq=seq, tt=tt, so=so: e.dma_start(
                out=xp_scr[seq * S + tt * 128: seq * S + (tt + 1) * 128, :],
                in_=xo[so][:, :]), waits=tev + [tz], inc=f"sxp{seq}")
            sl_xo[so].users += [tz, ts0, td]
            xp_w[(seq, tt)] = td
            z2toks.append(tfm)

        for ttg in range(NTG):
            ffn_block(seq, ttg,
                      lambda tt, seq=seq: (x_in[seq, tt * 128:(tt + 1) * 128, :], None),
                      lambda idx: st1m[:, idx:idx + 1], lambda idx: st1r[:, idx:idx + 1],
                      ricol[:, 0:1], alcol[:, 0:1], t_brow1, t_ri1c, t_al1c, ffn1_evac)

        attn_toks = []
        avpar = 0
        ptc = 0
        attn_first = True
        for hp in range(KM):
            swk = 0
            srck = wscr["wk"].rearrange("(k r) c -> r k c", r=128)[:, :, hp * 128:(hp + 1) * 128]
            dstk = wr[swk][:, :].rearrange("p (k c) -> p k c", k=KM)
            twlk = p.dma("sync", lambda e, srck=srck, dstk=dstk: e.dma_start(
                out=dstk, in_=srck),
                waits=sl_wr[swk].take() + [wscr_tok["wk"][k] for k in range(KM)],
                inc=f"wld{swk}")
            mmk = [None] * NQB
            for km in range(KM):
                for t512 in range(NQB):
                    b = t512
                    bw = sl_bank[b].take() if km == 0 else []
                    mmk[b] = p.op("tensor", lambda e, km=km, swk=swk, t512=t512, b=b: e.matmul(
                        bank[b][:, :], wr[swk][:, km * 128:(km + 1) * 128],
                        z2[:, km * S + t512 * 512: km * S + (t512 + 1) * 512],
                        start=(km == 0), stop=(km == KM - 1)),
                        waits=[twlk] + z2toks + bw, inc="ts")
                    sl_wr[swk].users.append(mmk[b])
                    sl_z2.users.append(mmk[b])
            kev = []
            ktw = sl_ktmp.take()
            for t512 in range(NQB):
                te = p.op("vector", lambda e, t512=t512: e.tensor_copy(
                    ktmp[:, t512 * 512:(t512 + 1) * 512], bank[t512][:, :]),
                    waits=[mmk[t512]] + (ktw if t512 == 0 else []), inc="vs")
                sl_bank[t512].users.append(te)
                kev.append(te)
            kf = hp % 2
            kfw = sl_kfm[kf].take()
            tkf = None
            for sl_ in range(NQB):
                lo = sl_ * 512
                tk1 = p.op("gpsimd", lambda e, lo=lo: e.tensor_tensor(
                    kropet[:, :], ktmp[:, lo:lo + 512], costab[:, lo:lo + 512], ALU.mult),
                    waits=kev + [t_cos]
                    + ([tkf] if tkf else ([krope_prev[0]] if krope_prev[0] else []))
                    + (kfw if sl_ == 0 else []), inc="gs")
                tks = []
                for g in range(4):
                    gs = [1, 0, 3, 2][g]
                    tks.append(p.op("gpsimd", lambda e, g=g, gs=gs, lo=lo: e.tensor_tensor(
                        kropes[g * 32:(g + 1) * 32, :],
                        ktmp[gs * 32:(gs + 1) * 32, lo:lo + 512],
                        sintab[gs * 32:(gs + 1) * 32, lo:lo + 512], ALU.mult),
                        waits=[tk1, t_sin], inc="gs"))
                tkf = p.op("gpsimd", lambda e, kf=kf, lo=lo: e.tensor_tensor(
                    kfm[kf][:, lo:lo + 512], kropet[:, :], kropes[:, :],
                    ALU.add), waits=tks, inc="gs")
                sl_ktmp.users.append(tkf)
            krope_prev[0] = tkf
            krope_tok = tkf

            swv = 1
            srcv = wscr["wv"].rearrange("(k r) c -> r k c", r=128)[:, :, hp * 128:(hp + 1) * 128]
            dstv = wr[swv][:, :].rearrange("p (k c) -> p k c", k=KM)
            twlv = p.dma("sync", lambda e, srcv=srcv, dstv=dstv: e.dma_start(
                out=dstv, in_=srcv),
                waits=sl_wr[swv].take() + [wscr_tok["wv"][k] for k in range(KM)],
                inc=f"wld{swv}")
            vf = hp % 2
            vfw = sl_vp[vf].take()
            for t in range(NT):
                b = 4 + (t % 2)
                mmv = None
                bw = sl_bank[b].take()
                for km in range(KM):
                    mmv = p.op("tensor", lambda e, km=km, swv=swv, t=t, b=b: e.matmul(
                        bank[b][:, 0:128], z2[:, km * S + t * 128: km * S + (t + 1) * 128],
                        wr[swv][:, km * 128:(km + 1) * 128],
                        start=(km == 0), stop=(km == KM - 1)),
                        waits=[twlv] + z2toks + (bw if km == 0 else []), inc="ts")
                    sl_wr[swv].users.append(mmv)
                    sl_z2.users.append(mmv)
                tev = p.op("vector", lambda e, t=t, b=b, vf=vf: e.tensor_copy(
                    vpair[vf][:, t * 128:(t + 1) * 128], bank[b][:, 0:128]),
                    waits=[mmv] + (vfw if t == 0 else []), inc="vs")
                sl_bank[b].users.append(tev)
                sl_vp[vf].users.append(tev)
                if t == 0:
                    v_toks = []
                v_toks.append(tev)

            sw = 0
            srcw_ap = wscr["wq"].rearrange("(k r) c -> r k c", r=128)[:, :, hp * 128:(hp + 1) * 128]
            dstw_ap = wr[sw][:, :].rearrange("p (k c) -> p k c", k=KM)
            twl = p.dma("sync", lambda e, srcw_ap=srcw_ap, dstw_ap=dstw_ap: e.dma_start(
                out=dstw_ap, in_=srcw_ap),
                waits=sl_wr[sw].take() + [wscr_tok["wq"][k] for k in range(KM)], inc=f"wld{sw}")
            mm_last = [None] * NQB
            for km in range(KM):
                for t512 in range(NQB):
                    b = t512
                    bw = sl_bank[b].take() if km == 0 else []
                    mm_last[b] = p.op("tensor", lambda e, km=km, sw=sw, t512=t512, b=b: e.matmul(
                        bank[b][:, :], wr[sw][:, km * 128:(km + 1) * 128],
                        z2[:, km * S + t512 * 512: km * S + (t512 + 1) * 512],
                        start=(km == 0), stop=(km == KM - 1)),
                        waits=[twl] + z2toks + bw, inc="ts")
                    sl_wr[sw].users.append(mm_last[b])
                    sl_z2.users.append(mm_last[b])
            shp = hp % 2
            tqf = None
            qhw = sl_qhp[shp].take()
            for sl_ in range(NQB):
                lo = sl_ * 512
                tq1 = p.op("vector", lambda e, sl_=sl_, lo=lo: e.tensor_tensor(
                    ropet[:, :], bank[sl_][:, :], costab[:, lo:lo + 512], ALU.mult),
                    waits=[mm_last[sl_], t_cos]
                    + ([tqf] if tqf else ([prev_tqf[0]] if prev_tqf[0] else [])), inc="vs")
                tqs = []
                for g in range(4):
                    gsw = [1, 0, 3, 2][g]
                    tqs.append(p.op("vector", lambda e, sl_=sl_, g=g, gsw=gsw, lo=lo: e.tensor_tensor(
                        ropes[g * 32:(g + 1) * 32, :], bank[sl_][gsw * 32:(gsw + 1) * 32, :],
                        sintab[gsw * 32:(gsw + 1) * 32, lo:lo + 512], ALU.mult),
                        waits=[tq1, t_sin], inc="vs"))
                tqf = p.op("vector", lambda e, shp=shp, lo=lo: e.tensor_tensor(
                    qhp[shp][:, lo:lo + 512], ropet[:, :], ropes[:, :], ALU.add),
                    waits=tqs + (qhw if sl_ == 0 else []), inc="vs")
                sl_bank[sl_].users += [tqf]
            prev_tqf[0] = tqf

            for h2 in range(2):
                head = 2 * hp + h2
                for qb in range(NQB):
                    nkc = (qb + 1) * 4
                    avb = 6 if avpar == 0 else 2
                    azb = 7 if avpar == 0 else 3
                    avw = sl_bank[avb].take()
                    azw = sl_bank[azb].take()
                    av_mm = None
                    for kc in range(nkc):
                        sb_ = 4 + (kc % 2)
                        sw_b = sl_bank[sb_].take()
                        tmm = p.op("tensor", lambda e, h2=h2, kc=kc, qb=qb, sb_=sb_, shp=shp, kf=kf: e.matmul(
                            bank[sb_][:, :],
                            kfm[kf][h2 * 64:(h2 + 1) * 64, kc * 128:(kc + 1) * 128],
                            qhp[shp][h2 * 64:(h2 + 1) * 64, qb * 512:(qb + 1) * 512],
                            start=True, stop=True),
                            waits=[tqf, krope_tok] + sw_b, inc="ts")
                        sl_qhp[shp].users.append(tmm)
                        sl_kfm[kf].users.append(tmm)
                        pt = ptc % 3
                        ptc += 1
                        texp = p.op("scalar", lambda e, sb_=sb_, pt=pt: e.activation(
                            ptile[pt][:, :], bank[sb_][:, :], AF.Exp,
                            bias=cshift[:, 0:1], scale=0.125),
                            waits=[tmm, t_csh] + sl_pt[pt].take(), inc="ss")
                        sl_bank[sb_].users.append(texp)
                        tlast = texp
                        j = kc - 4 * qb
                        if 0 <= j < 4:
                            tlast = p.op("vector", lambda e, pt=pt, j=j: e.tensor_tensor(
                                ptile[pt][:, :], ptile[pt][:, :],
                                masks[:, j * 512:(j + 1) * 512], ALU.mult),
                                waits=[texp, t_mask], inc="vs")
                        av_v = p.op("tensor", lambda e, kc=kc, head=head, h2=h2, pt=pt, avb=avb, nkc=nkc, vf=vf: e.matmul(
                            bank[avb][0:64, :],
                            vpair[vf][:, kc * 128 + h2 * 64: kc * 128 + (h2 + 1) * 64],
                            ptile[pt][:, :],
                            start=(kc == 0), stop=(kc == nkc - 1),
                            tile_position=(0, 0)),
                            waits=[tlast, v_toks[kc]] + (avw if kc == 0 else []),
                            inc="ts")
                        sl_vp[vf].users.append(av_v)
                        av_mm = p.op("tensor", lambda e, kc=kc, pt=pt, azb=azb, nkc=nkc: e.matmul(
                            bank[azb][64:128, :], ones64[:, :],
                            ptile[pt][:, :],
                            start=(kc == 0), stop=(kc == nkc - 1),
                            tile_position=(0, 64)),
                            waits=[av_v, t_vones] + (azw if kc == 0 else []), inc="ts")
                        sl_pt[pt].users.append(av_mm)
                    rp = avpar
                    trc = p.op("vector", lambda e, azb=azb, rp=rp: e.reciprocal(
                        recb[rp], bank[azb][64:128, :]),
                        waits=[av_mm] + sl_rec[rp].take(), inc="vs")
                    tat = p.op("vector", lambda e, avb=avb, rp=rp, h2=h2, hp=hp, qb=qb: e.tensor_tensor(
                        attn[h2 * 64:(h2 + 1) * 64, hp * S + qb * 512: hp * S + (qb + 1) * 512],
                        bank[avb][0:64, :], recb[rp], ALU.mult),
                        waits=[trc] + (sl_attn.take() if attn_first else []), inc="vs")
                    attn_first = False
                    sl_bank[azb].users.append(trc)
                    sl_bank[avb].users.append(tat)
                    sl_rec[rp].users.append(tat)
                    attn_toks.append(tat)
                    avpar = 1 - avpar

        for ttg in range(NTG):
            xp_toks = []
            for t4 in range(4):
                tt = ttg * 4 + t4
                slot = tt % NXR
                tl = p.dma("sync", lambda e, seq=seq, tt=tt, slot=slot: e.dma_start(
                    out=xr[slot][:, :],
                    in_=xp_scr[seq * S + tt * 128: seq * S + (tt + 1) * 128, :]),
                    waits=sl_xr[slot].take() + [xp_w[(seq, tt)]], inc=f"xin{slot}")
                xp_toks.append((slot, tl))
            mm_last = [None] * 8
            for kf in range(KM):
                sw = kf % 2
                twl = p.dma("sync", lambda e, kf=kf, sw=sw: e.dma_start(
                    out=wr[sw][:, :], in_=wscr["wo"][kf * 128:(kf + 1) * 128, :]),
                    waits=sl_wr[sw].take() + [wscr_tok["wo"][kf]], inc=f"wld{sw}")
                for t4 in range(4):
                    for oh in range(2):
                        b = t4 * 2 + oh
                        tt = ttg * 4 + t4
                        bw = sl_bank[b].take() if kf == 0 else []
                        mm = p.op("tensor", lambda e, kf=kf, sw=sw, tt=tt, oh=oh, b=b: e.matmul(
                            bank[b][:, :], attn[:, kf * S + tt * 128: kf * S + (tt + 1) * 128],
                            wr[sw][:, oh * 512:(oh + 1) * 512],
                            start=(kf == 0), stop=False),
                            waits=[twl] + attn_toks + bw, inc="ts")
                        mm_last[b] = mm
                        sl_wr[sw].users.append(mm)
                        sl_attn.users.append(mm)
            for t4 in range(4):
                for oh in range(2):
                    b = t4 * 2 + oh
                    mm_last[b] = p.op("tensor", lambda e, oh=oh, b=b: e.matmul(
                        bank[b][:, :], onesb[:, :], wob_row[0:1, oh * 512:(oh + 1) * 512],
                        start=False, stop=True),
                        waits=[mm_last[b], t_wob, t_onesb], inc="ts")
            for t4 in range(4):
                tt = ttg * 4 + t4
                idx = seq * NT + tt
                slot, tlx = xp_toks[t4]
                so = tt % 2
                tw_xo = sl_xo[so].take()
                tev = []
                for oh in range(2):
                    b = t4 * 2 + oh
                    tev.append(p.op("vector", lambda e, b=b, oh=oh, slot=slot, so=so: e.tensor_tensor(
                        xo[so][:, oh * 512:(oh + 1) * 512], bank[b][:, :],
                        xr[slot][:, oh * 512:(oh + 1) * 512], ALU.add),
                        waits=[mm_last[b], tlx] + (tw_xo if oh == 0 else []), inc="vs"))
                    sl_bank[b].users.append(tev[-1])
                sl_xr[slot].users += tev
                tv, tsc = ln_stats(xo[so][:, :], st3m[:, idx:idx + 1], st3r[:, idx:idx + 1], tev)
                tz = p.op("vector", lambda e, so=so, idx=idx: e.tensor_scalar(
                    yb[0][:, :], xo[so][:, :], st3m[:, idx:idx + 1], st3r[:, idx:idx + 1],
                    ALU.subtract, ALU.mult), waits=[tv] + sl_y[0].take(), inc="vs")
                tm = vsm(lambda e: e.tensor_reduce(
                    stt[:, 0:1], yb[0][:, :], AX.X, ALU.max, apply_absolute_value=True), [tz])
                sl_y[0].users.append(tm)
                last_macc[1] = vsm(lambda e: e.tensor_tensor(
                    maxacc[:, 1:2], maxacc[:, 1:2], stt[:, 0:1], ALU.max),
                    [tm] + ([last_macc[1]] if last_macc[1] else []))
                td = p.dma("gpsimd", lambda e, seq=seq, tt=tt, so=so: e.dma_start(
                    out=xpp_scr[seq * S + tt * 128: seq * S + (tt + 1) * 128, :],
                    in_=xo[so][:, :]), waits=tev + [tz], inc=f"sxpp{seq}")
                sl_xo[so].users += [tz, tsc, td]
                xpp_w[(seq, tt)] = td

    t_ri2c, t_al2c, t_brow2 = allreduce_scale(maxacc[:, 1:2], 1, ricol[:, 1:2], alcol[:, 1:2])

    for seq in range(SEQS):
        def ffn2_evac(t4, tt, slot, mm_last, al_ap, al_tok, seq=seq):
            so = tt % 2
            tw_xo = sl_xo[so].take()
            tev = []
            for oh in range(2):
                b = t4 * 2 + oh
                tev.append(p.op("vector", lambda e, b=b, oh=oh, slot=slot, so=so, al_ap=al_ap: e.scalar_tensor_tensor(
                    xo[so][:, oh * 512:(oh + 1) * 512], bank[b][:, :], al_ap,
                    xr[slot][:, oh * 512:(oh + 1) * 512], ALU.mult, ALU.add),
                    waits=[mm_last[b], al_tok] + (tw_xo if oh == 0 else []), inc="vs"))
                sl_bank[b].users.append(tev[-1])
            sl_xr[slot].users += tev
            td = p.dma("gpsimd", lambda e, seq=seq, tt=tt, so=so: e.dma_start(
                out=out[seq, tt * 128:(tt + 1) * 128, :], in_=xo[so][:, :]),
                waits=tev, inc="sout")
            sl_xo[so].users.append(td)

        for ttg in range(NTG):
            ffn_block(seq, ttg,
                      lambda tt, seq=seq: (
                          xpp_scr[seq * S + tt * 128: seq * S + (tt + 1) * 128, :],
                          xpp_w[(seq, tt)]),
                      lambda idx: st3m[:, idx:idx + 1], lambda idx: st3r[:, idx:idx + 1],
                      ricol[:, 1:2], alcol[:, 1:2], t_brow2, t_ri2c, t_al2c, ffn2_evac)

    if DBG:
        dbg = {}
        for nm, sh in [("dbg_xp", [SEQS * S, D]), ("dbg_xpp", [SEQS * S, D])]:
            dbg[nm] = nc.dram_tensor(nm, sh, F32, kind="ExternalOutput").ap()
        dbg_sb = {}
        for nm, t, sh in [("dbg_scal", scal, [1, 16]), ("dbg_ccres", ccres, [1, 32]),
                          ("dbg_st1m", st1m, [128, SEQS * NT]),
                          ("dbg_st1r", st1r, [128, SEQS * NT]),
                          ("dbg_ricol", ricol, [128, 2]), ("dbg_alcol", alcol, [128, 2]),
                          ("dbg_maxacc", maxacc, [128, 2])]:
            a = nc.dram_tensor(nm, sh, F32, kind="ExternalOutput").ap()
            p.dma("gpsimd", lambda e, a=a, t=t: e.dma_start(out=a[:, :], in_=t[:, :]),
                  waits=[p.cur(s_) for s_ in list(p.counts) if s_.startswith(("sout", "sxp", "swr", "gld"))] + [p.cur("vs"), p.cur("gs"), p.cur("ts")], inc="swr")
        for nm, t, sh in [("dbg_brow", brow, [1, D]), ("dbg_qbf", qbf[1], [128, D])]:
            a = nc.dram_tensor(nm, sh, BF16, kind="ExternalOutput").ap()
            p.dma("gpsimd", lambda e, a=a, t=t: e.dma_start(out=a[:, :], in_=t[:, :]),
                  waits=[p.cur(s_) for s_ in list(p.counts) if s_.startswith(("sout", "sxp", "swr", "gld"))] + [p.cur("vs"), p.cur("gs"), p.cur("ts")], inc="swr")
        ccdbg = nc.dram_tensor("dbg_cc", [4, 16], F32, kind="ExternalOutput").ap()
        for j, buf in enumerate([cci[0], cco[0], cci[1], cco[1]]):
            tl = p.dma("sync", lambda e, buf=buf: e.dma_start(
                out=mx128[0:1, 0:16], in_=buf.ap()[:, :]),
                waits=[p.cur(s_) for s_ in list(p.counts) if s_.startswith(("sout", "sxp", "swr", "gld"))] + [p.cur("cc"), p.cur("vs")], inc="gld")
            p.dma("gpsimd", lambda e, j=j: e.dma_start(
                out=ccdbg[j:j + 1, :], in_=mx128[0:1, 0:16]), waits=[tl], inc="swr")
        qwt_dbg = nc.dram_tensor("dbg_qwt", [D, D], BF16, kind="ExternalOutput").ap()
        for i in range(KM):
            tl = p.dma("sync", lambda e, i=i: e.dma_start(
                out=wr[0][:, :], in_=wscr["qwT"][i * 128:(i + 1) * 128, :]),
                waits=[p.cur(s_) for s_ in list(p.counts) if s_.startswith(("sout", "sxp", "swr", "gld"))] + [p.cur("ts")] + sl_wr[0].take(), inc="wld0")
            td = p.dma("gpsimd", lambda e, i=i: e.dma_start(
                out=qwt_dbg[i * 128:(i + 1) * 128, :], in_=wr[0][:, :]),
                waits=[tl], inc="swr")
            sl_wr[0].users.append(td)
        for nm, t, sh in [("dbg_z2", z2, [128, KM * S]), ("dbg_kfm", kfm, [128, KM * S]),
                          ("dbg_attn", attn, [128, KM * S]), ("dbg_q7", qhp[1], [128, S])]:
            a = nc.dram_tensor(nm, sh, BF16, kind="ExternalOutput").ap()
            p.dma("gpsimd", lambda e, a=a, t=t: e.dma_start(out=a[:, :], in_=t[:, :]),
                  waits=[p.cur(s_) for s_ in list(p.counts) if s_.startswith(("sout", "sxp", "swr", "gld"))] + [p.cur("vs"), p.cur("gs"), p.cur("ts")], inc="swr")
        for nm, srct in [("dbg_xp", xp_scr), ("dbg_xpp", xpp_scr)]:
            for i in range(SEQS * NT):
                tl = p.dma("sync", lambda e, srct=srct, i=i: e.dma_start(
                    out=xr[0][:, :], in_=srct[i * 128:(i + 1) * 128, :]),
                    waits=[p.cur(s_) for s_ in list(p.counts) if s_.startswith(("sout", "sxp", "swr", "gld"))] + [p.cur("vs")] + sl_xr[0].take(), inc="xin0")
                td = p.dma("gpsimd", lambda e, nm=nm, i=i: e.dma_start(
                    out=dbg[nm][i * 128:(i + 1) * 128, :], in_=xr[0][:, :]),
                    waits=[tl], inc="swr")
                sl_xr[0].users.append(td)

    p.op("gpsimd", lambda e: e.nop(), waits=[p.cur(s_) for s_ in list(p.counts) if s_.startswith(("sout", "sxp", "swr", "gld"))])

    sems = {}
    for s_ in p.counts:
        sems[s_] = es.enter_context(nc.semaphore(s_))
    with nc.Block() as block:
        @block.sync
        def _(eng):
            p.emit("sync", eng, sems)

        @block.vector
        def _(eng):
            p.emit("vector", eng, sems)

        @block.scalar
        def _(eng):
            p.emit("scalar", eng, sems)

        @block.tensor
        def _(eng):
            p.emit("tensor", eng, sems)

        @block.gpsimd
        def _(eng):
            p.emit("gpsimd", eng, sems)

    es.close()
    return nc


def kernel(L0_attn_x, L0_attn_wq, L0_attn_wk, L0_attn_wv, L0_attn_wo_w,
           L0_attn_wo_b, L0_ffn_weight, L0_ffn_bias):
    B, S, D_ = L0_attn_x.shape
    assert D_ == D and B % NCORES == 0
    SEQS = B // NCORES
    x = np.ascontiguousarray(np.asarray(L0_attn_x, dtype=np.float32))
    perm = rope_perm()
    wq = np.ascontiguousarray(np.asarray(L0_attn_wq, np.float32)[:, perm])
    wk = np.ascontiguousarray(np.asarray(L0_attn_wk, np.float32)[:, perm])
    wv = np.ascontiguousarray(np.asarray(L0_attn_wv, np.float32))
    wo = np.ascontiguousarray(np.asarray(L0_attn_wo_w, np.float32))
    wob = np.ascontiguousarray(np.asarray(L0_attn_wo_b, np.float32)).reshape(1, D)
    fw = np.ascontiguousarray(np.asarray(L0_ffn_weight, np.float32))
    fb = np.ascontiguousarray(np.asarray(L0_ffn_bias, np.float32)).reshape(1, D)
    costab, sintab, masks = host_tables(S)
    id32 = np.eye(128, dtype=np.float32)
    idbf = np.eye(128, dtype=ml_dtypes.bfloat16)
    ccol_c = np.zeros((128, 16), np.float32)
    ccol_c[:, 0] = 1.0 / D
    ccol_c[:, 1] = EPS
    ccol_c[:, 2] = 3.0
    ccol_c[:, 3] = -0.5
    ccol_c[:, 4] = C_SHIFT
    ccol_c[:, 5] = -0.5
    crow_c = np.zeros((1, 16), np.float32)
    crow_c[0, :5] = [2.0, 1.0 / (D * D), 0.5, 1.0 / 127.0, -1.0]
    onesf_c = np.ones((1, 128), np.float32)
    onesb_c = np.ones((1, 128), ml_dtypes.bfloat16)
    vtones_c = np.ones((128, 64), ml_dtypes.bfloat16)

    nc = build_nc(SEQS, S)
    in_maps = []
    for c in range(NCORES):
        in_maps.append({
            "x": x[c * SEQS:(c + 1) * SEQS],
            "wq": wq, "wk": wk, "wv": wv, "wo": wo,
            "wob": wob, "fw": fw, "fb": fb,
            "id32": id32, "idbf": idbf,
            "costab": costab, "sintab": sintab, "masks": masks,
            "ccol_c": ccol_c, "crow_c": crow_c, "onesf_c": onesf_c,
            "onesb_c": onesb_c, "vtones_c": vtones_c,
        })
    res = run_bass_kernel_spmd(nc, in_maps, core_ids=list(range(NCORES)))
    if DBG:
        return res
    return np.concatenate([r["out"] for r in res.results], axis=0)


# version 11
# speedup vs baseline: 1.3292x; 1.3292x over previous
import numpy as np
import ml_dtypes
from contextlib import ExitStack
from dataclasses import replace as dc_replace

import concourse.bass as bass
import concourse.mybir as mybir
from concourse.bass_utils import run_bass_kernel_spmd

F32 = mybir.dt.float32
BF16 = mybir.dt.bfloat16
I32 = mybir.dt.int32
AF = mybir.ActivationFunctionType
ALU = mybir.AluOpType
AX = mybir.AxisListType

NCORES = 8
D = 1024
H = 16
HD = 64
KM = D // 128
EPS = 1e-5
C_SHIFT = -8.0
DBG = False
ALIAS_ROPE = True
ALIAS_W = True


class Prog:

    def __init__(self):
        self.ops = []
        self.counts = {}
        self.history = {}

    def sem(self, name):
        self.counts.setdefault(name, 0)
        self.history.setdefault(name, [])

    def op(self, eng, fn, waits=(), inc=None, amt=1, dma=False):
        tok = None
        if inc is not None:
            self.sem(inc)
        if inc is not None:
            self.counts[inc] += amt
            self.history[inc].append((self.counts[inc], eng, dma))
            tok = (inc, self.counts[inc])
        ws = []
        for w in waits:
            if w is None:
                continue
            if isinstance(w, list):
                ws += [x for x in w if x is not None]
            else:
                ws.append(w)
        self.ops.append((eng, ws, (inc, amt) if inc else None, fn))
        return tok

    def dma(self, eng, fn, waits=(), inc="swr"):
        return self.op(eng, fn, waits, inc, amt=16, dma=True)

    def cur(self, sem):
        return (sem, self.counts[sem])

    def emit(self, engname, eng, sems):
        dma_sems = {s for s, hist in self.history.items()
                    if any(isdma for (_, _, isdma) in hist)}
        watermark = {}
        prefix = {s: 0 for s in self.counts}
        for (e, waits, inc, fn) in self.ops:
            if e == engname:
                merged = {}
                for s, v in waits:
                    if s in dma_sems:
                        v = prefix[s]
                    if v > merged.get(s, 0):
                        merged[s] = v
                for s, v in sorted(merged.items()):
                    if watermark.get(s, 0) >= v:
                        continue
                    eng.wait_ge(sems[s], v)
                    watermark[s] = v
                ins = fn(eng)
                if inc is not None:
                    ins.then_inc(sems[inc[0]], inc[1])
            if inc is not None:
                prefix[inc[0]] += inc[1]


class Slot:
    def __init__(self):
        self.users = []

    def take(self):
        u, self.users = self.users, []
        return u


def host_tables(S):
    half = HD // 2
    inv = 1.0 / (10000.0 ** (np.arange(0, HD, 2)[:half].astype(np.float64) / HD))
    ang = np.outer(np.arange(S, dtype=np.float64), inv)
    cos = np.cos(ang).astype(np.float32).T
    sin = np.sin(ang).astype(np.float32).T
    costab = np.empty((128, S), np.float32)
    sintab = np.empty((128, S), np.float32)
    for g in range(4):
        costab[g * 32:(g + 1) * 32] = cos
        sintab[g * 32:(g + 1) * 32] = sin * (1.0 if g % 2 == 0 else -1.0)
    masks = np.empty((128, 2048), np.float32)
    kk = np.arange(128)[:, None]
    qq = np.arange(512)[None, :]
    for j in range(4):
        masks[:, j * 512:(j + 1) * 512] = (qq >= kk + j * 128)
    bf = ml_dtypes.bfloat16
    return costab.astype(bf), sintab.astype(bf), masks.astype(bf)


def rope_perm():
    pm = []
    for h in range(H):
        base = h * HD
        pm += [base + i for i in range(0, HD, 2)]
        pm += [base + i for i in range(1, HD, 2)]
    return np.array(pm, np.int64)


def vt_head_ap(vt_t, head):
    full = vt_t[:, :]
    blockstep = (H * HD) - head * HD
    new_dims = [list(full.ap[0]), [blockstep, 2], [1, HD]]
    return dc_replace(full, offset=full.offset + head * HD, ap=new_dims)


def build_nc(SEQS, S):
    NT = S // 128
    NTG = NT // 4
    NQB = S // 512

    nc = bass.Bass("TRN2", target_bir_lowering=False, debug=False)

    x_in = nc.dram_tensor("x", [SEQS, S, D], F32, kind="ExternalInput").ap()
    wq_in = nc.dram_tensor("wq", [D, D], F32, kind="ExternalInput").ap()
    wk_in = nc.dram_tensor("wk", [D, D], F32, kind="ExternalInput").ap()
    wv_in = nc.dram_tensor("wv", [D, D], F32, kind="ExternalInput").ap()
    wo_in = nc.dram_tensor("wo", [D, D], F32, kind="ExternalInput").ap()
    wob_in = nc.dram_tensor("wob", [1, D], F32, kind="ExternalInput").ap()
    fw_in = nc.dram_tensor("fw", [D, D], F32, kind="ExternalInput").ap()
    fb_in = nc.dram_tensor("fb", [1, D], F32, kind="ExternalInput").ap()
    id32_in = nc.dram_tensor("id32", [128, 128], F32, kind="ExternalInput").ap()
    idbf_in = nc.dram_tensor("idbf", [128, 128], BF16, kind="ExternalInput").ap()
    cos_in = nc.dram_tensor("costab", [128, S], BF16, kind="ExternalInput").ap()
    sin_in = nc.dram_tensor("sintab", [128, S], BF16, kind="ExternalInput").ap()
    mask_in = nc.dram_tensor("masks", [128, 2048], BF16, kind="ExternalInput").ap()
    ccol_in = nc.dram_tensor("ccol_c", [128, 32], F32, kind="ExternalInput").ap()
    crow_in = nc.dram_tensor("crow_c", [1, 16], F32, kind="ExternalInput").ap()
    onesf_in = nc.dram_tensor("onesf_c", [1, 128], F32, kind="ExternalInput").ap()
    onesb_in = nc.dram_tensor("onesb_c", [1, 128], BF16, kind="ExternalInput").ap()
    vtones_in = nc.dram_tensor("vtones_c", [128, 64], BF16, kind="ExternalInput").ap()
    out = nc.dram_tensor("out", [SEQS, S, D], F32, kind="ExternalOutput").ap()

    xp_scr = nc.dram_tensor("xp_scr", [SEQS * S, D], F32).ap()
    xpp_scr = nc.dram_tensor("xpp_scr", [SEQS * S, D], F32).ap()
    wscr = {w: nc.dram_tensor(f"{w}_scr", [D, D], BF16).ap()
            for w in ["wq", "wk", "wv", "wo", "qwT"]}
    cci = [nc.dram_tensor(f"cc{i}_in", [1, 16], F32) for i in range(2)]
    cco = [nc.dram_tensor(f"cc{i}_out", [1, 16], F32) for i in range(2)]

    es = ExitStack()
    sbuf = lambda n, sh, dt: es.enter_context(nc.sbuf_tensor(n, sh, dt))
    psumt = lambda n, sh, dt: es.enter_context(nc.psum_tensor(n, sh, dt))

    id32 = sbuf("c_id32", [128, 128], F32)
    idbf = sbuf("c_idbf", [128, 128], BF16)
    costab = sbuf("c_cos", [128, S], BF16)
    sintab = sbuf("c_sin", [128, S], BF16)
    masks = sbuf("c_masks", [128, 2048], BF16)
    onesf = sbuf("c_onesf", [1, 128], F32)
    onesb = sbuf("c_onesb", [1, 128], BF16)
    wob_row = sbuf("c_wobrow", [1, D], BF16)
    brow = sbuf("c_brow", [1, D], BF16)

    NXR = 4
    xr = [sbuf(f"b_xr{i}", [128, D], F32) for i in range(NXR)]
    xo = [sbuf(f"b_xo{i}", [128, D], F32) for i in range(4)]
    yb = [sbuf(f"b_y{i}", [128, D], F32) for i in range(2)]
    ib = [sbuf(f"b_i{i}", [128, D], I32) for i in range(2)]
    qab0 = sbuf("b_qa0", [128, D], BF16)
    qab = [qab0, qab0]
    c1b = sbuf("b_c1", [128, D], F32)
    c2b = c1b
    qbf = [sbuf(f"b_qbf{i}", [128, D], BF16) for i in range(2)]
    qfm = [[sbuf(f"b_qfm{g}_{i}", [128, D], BF16) for i in range(4)] for g in range(2)]

    st1m = sbuf("s_st1m", [128, SEQS * NT], F32)
    st1r = sbuf("s_st1r", [128, SEQS * NT], F32)
    st3m = sbuf("s_st3m", [128, SEQS * NT], F32)
    st3r = sbuf("s_st3r", [128, SEQS * NT], F32)
    ricol = sbuf("s_ricol", [128, 2], F32)
    alcol = sbuf("s_alcol", [128, 2], F32)
    thrcol = sbuf("s_thrcol", [128, 1], F32)
    acol = sbuf("s_acol", [128, 2], F32)
    l2m = sbuf("s_l2m", [128, 4], F32)
    l2r = sbuf("s_l2r", [128, 4], F32)
    scal = sbuf("s_scal", [1, 16], F32)
    ccres = sbuf("s_ccres", [1, 32], F32)
    stt = sbuf("s_stt", [128, 24], F32)
    sqs = sbuf("s_sqs", [128, D], BF16)
    z2tm = sqs
    maxacc = sbuf("s_maxacc", [128, 2], F32)
    cshift = sbuf("s_cshift", [128, 1], F32)
    chalf = sbuf("s_chalf", [128, 1], F32)
    ccol = sbuf("s_ccol", [128, 32], F32)
    crow = sbuf("s_crow", [1, 16], F32)
    mx128 = sbuf("s_mx128", [1, 128], F32)

    z2 = sbuf("a_z2", [128, KM * S], BF16)
    kfm = [sbuf(f"a_kfm{i}", [128, S], BF16) for i in range(2)]
    vpair = [sbuf(f"a_vp{i}", [128, S], BF16) for i in range(2)]
    ones64 = sbuf("a_ones64", [128, HD], BF16)
    attn = sbuf("a_attn", [128, KM * S], BF16)
    ktmp = sbuf("a_ktmp", [128, S], BF16)
    qhp = [sbuf(f"a_qhp{i}", [128, S], BF16) for i in range(2)]
    ropet = sbuf("a_ropet", [128, 512], BF16)
    ropes = sbuf("a_ropes", [128, 512], BF16)
    kropet = sbuf("a_kropet", [128, 512], BF16)
    kropes = sbuf("a_kropes", [128, 512], BF16)
    ptile = [sbuf(f"a_pt{i}", [128, 512], BF16) for i in range(3)]
    recb2 = sbuf("a_rec", [128, 512], F32)
    recb = [recb2[0:64, :], recb2[64:128, :]]

    wr = [sbuf(f"w_r{i}", [128, D], BF16) for i in range(2)]
    wstage = xo[0:2]
    absw = c1b
    wcast = sbuf("w_cast", [128, D], BF16)
    sgnw = qab0
    gtw = sbuf("w_gtw", [128, D], BF16)
    qwc = sbuf("w_qwc", [128, D], BF16)
    qwt_ev = qbf[0]

    bank = [psumt(f"p_b{i}", [128, 512], F32) for i in range(8)]

    p = Prog()
    for s_ in ["ld", "ts", "vs", "ss", "gs", "cc", "sout", "swr", "gld"]:
        p.sem(s_)

    last_macc = [None, None]
    prev_tqf = [None]
    krope_prev = [None]
    sl_xr = [Slot() for _ in range(NXR)]
    sl_xo = [Slot() for _ in range(4)]
    sl_y = [Slot() for _ in range(2)]
    sl_i = [Slot() for _ in range(2)]
    _sl_qa0 = Slot()
    sl_qa = [_sl_qa0, _sl_qa0]
    sl_qbf = [Slot() for _ in range(2)]
    sl_qfm = [[Slot() for _ in range(4)] for _ in range(2)]
    sl_wr = [Slot() for _ in range(2)]
    sl_wst = [Slot() for _ in range(2)]
    sl_wcast = Slot()
    sl_bank = [Slot() for _ in range(8)]
    sl_pt = [Slot() for _ in range(3)]
    sl_rec = [Slot() for _ in range(2)]
    sl_qhp = [Slot() for _ in range(2)]
    sl_c1 = Slot()
    sl_c2 = sl_c1
    sl_u1 = Slot()
    sl_u2 = Slot()
    sl_r1 = Slot()
    sl_r2 = Slot()
    sl_absw = Slot()
    sl_qwc = Slot()
    sl_qwtev = Slot()
    sl_rope = Slot()
    sl_sttq = Slot()
    sl_stt4 = Slot()
    sl_z2 = Slot()
    sl_kfm = [Slot() for _ in range(2)]
    sl_attn = Slot()
    sl_ktmp = Slot()
    sl_vp = [Slot() for _ in range(2)]
    sl_z2tm = Slot()
    wscr_tok = {k: {} for k in wscr}
    xp_w = {}
    xpp_w = {}

    def vsm(fn, waits):
        t_ = p.op("vector", fn, waits, inc="vs")
        return p.op("vector", lambda e: e.drain(), [t_], inc="vs")

    def ssm(fn, waits):
        t_ = p.op("scalar", fn, waits, inc="ss")
        return p.op("scalar", lambda e: e.drain(), [t_], inc="ss")

    def bcast_col(src11, dstcol, waits):
        bw = sl_bank[7].take()
        t1 = p.op("tensor", lambda e: e.matmul(bank[7][:, 0:1], onesf[:, :], src11,
                                               start=True, stop=True),
                  waits=list(waits) + bw, inc="ts")
        t2 = vsm(lambda e: e.tensor_copy(dstcol, bank[7][:, 0:1]), [t1])
        sl_bank[7].users.append(t2)
        return t2

    def part_reduce_max(srccol, dst11, waits):
        bw = sl_bank[7].take()
        t1 = p.op("tensor", lambda e: e.transpose(bank[7][0:1, 0:128], srccol, id32[:, :]),
                  waits=list(waits) + bw, inc="ts")
        t2 = vsm(lambda e: e.tensor_copy(mx128[:, :], bank[7][0:1, 0:128]), [t1])
        sl_bank[7].users.append(t2)
        t3 = vsm(lambda e: e.tensor_reduce(dst11, mx128[:, :], AX.X, ALU.max), [t2])
        return t3

    def ln_stats(xap, mcol_ap, rcol_ap, waits_x):
        tv0 = vsm(lambda e: e.tensor_reduce(stt[:, 0:1], xap, AX.X, ALU.add), waits_x)
        ts0_ = p.op("scalar", lambda e: e.activation(sqs[:, :], xap, AF.Square,
                                                     accum_out=stt[:, 1:2]),
                    waits=list(waits_x) + sl_sttq.take() + sl_z2tm.take(), inc="ss")
        ts0 = p.op("scalar", lambda e: e.drain(), [ts0_], inc="ss")
        t1 = vsm(lambda e: e.tensor_tensor(
            mcol_ap, stt[:, 0:1], ccol[:, 0:1], ALU.mult), [tv0] + t_cc0)
        t2 = vsm(lambda e: e.tensor_tensor(
            stt[:, 2:3], mcol_ap, mcol_ap, ALU.mult), [t1])
        t3 = vsm(lambda e: e.tensor_tensor(
            stt[:, 3:4], stt[:, 1:2], ccol[:, 0:1], ALU.mult), [ts0])
        sl_sttq.users.append(t3)
        t4 = vsm(lambda e: e.tensor_tensor(
            stt[:, 3:4], stt[:, 3:4], stt[:, 2:3], ALU.subtract), [t2, t3])
        t5 = vsm(lambda e: e.tensor_tensor(
            stt[:, 3:4], stt[:, 3:4], ccol[:, 1:2], ALU.add), [t4])
        t6a = ssm(lambda e: e.activation(stt[:, 5:6], stt[:, 3:4], AF.Sqrt),
                  [t5] + sl_stt4.take())
        t6 = vsm(lambda e: e.reciprocal(stt[:, 4:5], stt[:, 5:6]), [t6a])
        t7 = vsm(lambda e: e.tensor_tensor(
            stt[:, 5:6], stt[:, 4:5], stt[:, 4:5], ALU.mult), [t6])
        t8 = vsm(lambda e: e.tensor_tensor(
            stt[:, 5:6], stt[:, 5:6], stt[:, 3:4], ALU.mult), [t7])
        t9a = vsm(lambda e: e.tensor_tensor(
            stt[:, 5:6], stt[:, 5:6], ccol[:, 2:3], ALU.subtract), [t8])
        t9 = vsm(lambda e: e.tensor_tensor(
            stt[:, 5:6], stt[:, 5:6], ccol[:, 3:4], ALU.mult), [t9a])
        t10 = vsm(lambda e: e.tensor_tensor(
            rcol_ap, stt[:, 5:6], stt[:, 4:5], ALU.mult), [t9])
        sl_stt4.users.append(t10)
        return t10, ts0

    def newton_recip(src11, dst11, waits):
        t1 = vsm(lambda e: e.reciprocal(dst11, src11), waits)
        t2 = vsm(lambda e: e.tensor_tensor(scal[0:1, 14:15], dst11, src11, ALU.mult), [t1])
        t3 = vsm(lambda e: e.tensor_tensor(
            scal[0:1, 14:15], crow[0:1, 0:1], scal[0:1, 14:15], ALU.subtract),
            [t2] + t_cc0)
        t4 = vsm(lambda e: e.tensor_tensor(dst11, dst11, scal[0:1, 14:15], ALU.mult), [t3])
        return t4

    def ln_stats4(xaps, mdest, rdest, waits_each):
        rts = []
        for t4 in range(4):
            rts.append(p.op("vector", lambda e, t4=t4: e.tensor_reduce(
                stt[:, t4:t4 + 1], xaps[t4], AX.X, ALU.add),
                waits=waits_each[t4], inc="vs"))
        trd = p.op("vector", lambda e: e.drain(), rts, inc="vs")
        sqs_toks = []
        for t4 in range(4):
            sqs_toks.append(p.op("scalar", lambda e, t4=t4: e.activation(
                sqs[:, :], xaps[t4], AF.Square, accum_out=stt[:, 4 + t4:5 + t4]),
                waits=list(waits_each[t4]) + sl_sttq.take() + sl_z2tm.take()
                + sqs_toks[-1:], inc="ss"))
        tsd = p.op("scalar", lambda e: e.drain(), sqs_toks, inc="ss")
        m4 = vsm(lambda e: e.tensor_tensor(
            stt[:, 8:12], stt[:, 0:4], ccol[:, 16:20], ALU.mult), [trd] + t_cc0)
        mc = vsm(lambda e: e.tensor_copy(mdest, stt[:, 8:12]), [m4])
        ex2 = vsm(lambda e: e.tensor_tensor(
            stt[:, 4:8], stt[:, 4:8], ccol[:, 16:20], ALU.mult), [tsd])
        sl_sttq.users.append(ex2)
        msq = vsm(lambda e: e.tensor_tensor(
            stt[:, 12:16], stt[:, 8:12], stt[:, 8:12], ALU.mult), [m4])
        ve0 = vsm(lambda e: e.tensor_tensor(
            stt[:, 4:8], stt[:, 4:8], stt[:, 12:16], ALU.subtract), [ex2, msq])
        ve = vsm(lambda e: e.tensor_tensor(
            stt[:, 4:8], stt[:, 4:8], ccol[:, 20:24], ALU.add), [ve0])
        sq4 = ssm(lambda e: e.activation(stt[:, 12:16], stt[:, 4:8], AF.Sqrt),
                  [ve] + sl_stt4.take())
        r0 = vsm(lambda e: e.reciprocal(stt[:, 8:12], stt[:, 12:16]), [sq4])
        n1 = vsm(lambda e: e.tensor_tensor(
            stt[:, 12:16], stt[:, 8:12], stt[:, 8:12], ALU.mult), [r0])
        n2 = vsm(lambda e: e.tensor_tensor(
            stt[:, 12:16], stt[:, 12:16], stt[:, 4:8], ALU.mult), [n1])
        n3 = vsm(lambda e: e.tensor_tensor(
            stt[:, 12:16], stt[:, 12:16], ccol[:, 24:28], ALU.subtract), [n2])
        n4 = vsm(lambda e: e.tensor_tensor(
            stt[:, 12:16], stt[:, 12:16], ccol[:, 28:32], ALU.mult), [n3])
        rr = vsm(lambda e: e.tensor_tensor(
            rdest, stt[:, 12:16], stt[:, 8:12], ALU.mult), [n4])
        sl_stt4.users.append(rr)
        return ([mc, rr], sqs_toks)

    def transpose_tile(src_ap, dst_ap, waits, strided_dst=False, bank_i=0):
        bw = sl_bank[bank_i].take()
        last = None
        for j in range(KM):
            last = p.op("tensor", lambda e, j=j: e.transpose(
                bank[bank_i].bitcast(BF16)[:, j * 128:(j + 1) * 128],
                src_ap[:, j * 128:(j + 1) * 128], idbf[:, :]),
                waits=(list(waits) + bw + [t_idbf]) if j == 0 else [], inc="ts")
        if strided_dst:
            src_ps = bank[bank_i].bitcast(BF16)[:, 0:1024].rearrange("p (j c) -> p j c", j=KM)
        else:
            src_ps = bank[bank_i].bitcast(BF16)[:, 0:1024]
        tev = p.op("vector", lambda e: e.tensor_copy(dst_ap, src_ps),
                   waits=[last], inc="vs")
        sl_bank[bank_i].users.append(tev)
        return tev, last

    def quant_chain(src_fn, m_ap, r_ap, ri_ap, src_users, extra_waits, sy):
        ta_ = p.op("vector", lambda e: e.tensor_tensor(
            acol[:, 0:1], r_ap, ri_ap, ALU.mult), waits=extra_waits, inc="vs")
        ta = p.op("vector", lambda e: e.drain(), waits=[ta_], inc="vs")
        ty = p.op("vector", lambda e: e.tensor_scalar(
            yb[sy][:, :], src_fn(), m_ap, acol[:, 0:1],
            ALU.subtract, ALU.mult), waits=[ta] + sl_y[sy].take(), inc="vs")
        src_users.append(ty)
        ti = p.op("vector", lambda e: e.tensor_copy(ib[sy][:, :], yb[sy][:, :]),
                  waits=[ti_w for ti_w in [ty]] + sl_i[sy].take(), inc="vs")
        tqa = p.op("vector", lambda e: e.tensor_copy(qab[sy][:, :], ib[sy][:, :]),
                   waits=[ti] + sl_qa[sy].take(), inc="vs")
        td_ = p.op("gpsimd", lambda e: e.tensor_tensor(
            c2b[:, :], yb[sy][:, :], ib[sy][:, :], ALU.subtract),
            waits=[ti] + sl_c2.take(), inc="gs")
        sl_y[sy].users.append(td_)
        sl_i[sy].users.append(td_)
        tu1 = p.op("scalar", lambda e: e.activation(
            gtw[:, :], c2b[:, :], AF.Sign, bias=chalf[:, 0:1], scale=1.0),
            waits=[td_, t_chf] + sl_u1.take(), inc="ss")
        tr1 = p.op("scalar", lambda e: e.activation(qwc[:, :], gtw[:, :], AF.Relu),
                   waits=[tu1] + sl_r1.take(), inc="ss")
        sl_u1.users.append(tr1)
        tu2 = p.op("scalar", lambda e: e.activation(
            gtw[:, :], c2b[:, :], AF.Sign, bias=chalf[:, 0:1], scale=-1.0),
            waits=[td_, tr1], inc="ss")
        sl_c2.users += [tu1, tu2]
        tr2 = p.op("scalar", lambda e: e.activation(wcast[:, :], gtw[:, :], AF.Relu),
                   waits=[tu2] + sl_r2.take(), inc="ss")
        sl_u1.users.append(tr2)
        te = p.op("gpsimd", lambda e: e.tensor_tensor(
            c1b[:, :], qwc[:, :], wcast[:, :], ALU.subtract),
            waits=[tr1, tr2] + sl_c1.take(), inc="gs")
        sl_r1.users.append(te)
        sl_r2.users.append(te)
        tq = p.op("vector", lambda e: e.tensor_tensor(
            qbf[sy][:, :], qab[sy][:, :], c1b[:, :], ALU.add),
            waits=[te, tqa] + sl_qbf[sy].take(), inc="vs")
        sl_qa[sy].users.append(tq)
        sl_c1.users.append(tq)
        return tq

    def ffn_block_seq(seq, src_of_tt, m_of, r_of, ri_ap, ri_tok, al_ap, al_tok,
                      brow_tok, post_fn):
        info = {}

        def stage_b_mm(tg):
            g = tg % 2
            slots4, tfm4 = info.pop(tg)
            ev_toks = [[] for _ in range(4)]
            for oh in range(2):
                mm4 = [None] * 4
                for km in range(KM):
                    sw = km % 2
                    twl = p.dma("sync", lambda e, km=km, oh=oh, sw=sw: e.dma_start(
                        out=wr[sw][:, 0:512],
                        in_=wscr["qwT"][km * 128:(km + 1) * 128, oh * 512:(oh + 1) * 512]),
                        waits=sl_wr[sw].take() + [wscr_tok["qwT"][km]], inc=f"wld{sw}")
                    for t4 in range(4):
                        b = 4 + t4
                        bw = sl_bank[b].take() if km == 0 else []
                        mm4[t4] = p.op("tensor", lambda e, km=km, sw=sw, g=g, t4=t4, b=b: e.matmul(
                            bank[b][:, :], qfm[g][t4][:, km * 128:(km + 1) * 128],
                            wr[sw][:, 0:512],
                            start=(km == 0), stop=False),
                            waits=[twl, tfm4[t4]] + bw, inc="ts")
                        sl_wr[sw].users.append(mm4[t4])
                        sl_qfm[g][t4].users.append(mm4[t4])
                for t4 in range(4):
                    b = 4 + t4
                    mm4[t4] = p.op("tensor", lambda e, oh=oh, b=b: e.matmul(
                        bank[b][:, :], onesb[:, :], brow[0:1, oh * 512:(oh + 1) * 512],
                        start=False, stop=True),
                        waits=[mm4[t4], brow_tok, t_onesb], inc="ts")
                for t4 in range(4):
                    b = 4 + t4
                    so = t4
                    tw_xo = sl_xo[so].take() if oh == 0 else []
                    tev = p.op("vector", lambda e, b=b, oh=oh, slot=slots4[t4], so=so, al_ap=al_ap: e.scalar_tensor_tensor(
                        xo[so][:, oh * 512:(oh + 1) * 512], bank[b][:, :], al_ap,
                        xr[slot][:, oh * 512:(oh + 1) * 512], ALU.mult, ALU.add),
                        waits=[mm4[t4], al_tok] + tw_xo, inc="vs")
                    sl_bank[b].users.append(tev)
                    sl_xr[slots4[t4]].users.append(tev)
                    ev_toks[t4].append(tev)
            return slots4, ev_toks

        for ttg in range(NTG + 1):
            pend_b = None
            if ttg >= 1:
                pend_b = stage_b_mm(ttg - 1)
            if ttg < NTG:
                g = ttg % 2
                slots4, tfm4 = [], []
                for t4 in range(4):
                    tt = ttg * 4 + t4
                    idx = seq * NT + tt
                    slot = idx % NXR
                    src, srcw = src_of_tt(tt)
                    tl = p.dma("sync", lambda e, src=src, slot=slot: e.dma_start(
                        out=xr[slot][:, :], in_=src),
                        waits=sl_xr[slot].take() + ([srcw] if srcw else []),
                        inc=f"xin{slot}")
                    tq = quant_chain(lambda slot=slot: xr[slot][:, :],
                                     m_of(idx), r_of(idx), ri_ap,
                                     sl_xr[slot].users, [tl, ri_tok], idx % 2)
                    tfm, _ = transpose_tile(qbf[idx % 2][:, :], qfm[g][t4][:, :],
                                            [tq] + sl_qfm[g][t4].take(), bank_i=t4)
                    sl_qbf[idx % 2].users.append(tfm)
                    slots4.append(slot)
                    tfm4.append(tfm)
                info[ttg] = (slots4, tfm4)
            if pend_b is not None:
                post_fn(ttg - 1, pend_b[0], pend_b[1])

    t_id32 = p.dma("sync", lambda e: e.dma_start(out=id32[:, :], in_=id32_in[:, :]), inc="ld")
    t_idbf = p.dma("sync", lambda e: e.dma_start(out=idbf[:, :], in_=idbf_in[:, :]), inc="ld")
    t_cos = p.dma("sync", lambda e: e.dma_start(out=costab[:, :], in_=cos_in[:, :]), inc="ld")
    t_sin = p.dma("sync", lambda e: e.dma_start(out=sintab[:, :], in_=sin_in[:, :]), inc="ld")
    t_mask = p.dma("sync", lambda e: e.dma_start(out=masks[:, :], in_=mask_in[:, :]), inc="ld")
    t_wob0 = p.dma("sync", lambda e: e.dma_start(out=wstage[0][0:1, :], in_=wob_in[:, :]), inc="wst0")
    t_ones = p.dma("sync", lambda e: e.dma_start(out=onesf[:, :], in_=onesf_in[:, :]), inc="ld")
    t_onesb = p.dma("sync", lambda e: e.dma_start(out=onesb[:, :], in_=onesb_in[:, :]), inc="ld")
    t_ccl = p.dma("sync", lambda e: e.dma_start(out=ccol[:, :], in_=ccol_in[:, :]), inc="ld")
    t_crl = p.dma("sync", lambda e: e.dma_start(out=crow[:, :], in_=crow_in[:, :]), inc="ld")
    t_mz = p.dma("sync", lambda e: e.dma_start(out=maxacc[:, :], in_=ccol_in[:, 6:8], allow_slow_non_contiguous=True), inc="ld")
    t_csh = p.dma("sync", lambda e: e.dma_start(out=cshift[:, :], in_=ccol_in[:, 4:5], allow_slow_non_contiguous=True), inc="ld")
    t_chf = p.dma("sync", lambda e: e.dma_start(out=chalf[:, :], in_=ccol_in[:, 5:6], allow_slow_non_contiguous=True), inc="ld")
    t_cc0 = [t_ccl, t_crl]
    t_wob = p.op("vector", lambda e: e.tensor_copy(wob_row[:, :], wstage[0][0:1, :]),
                 waits=[t_wob0], inc="vs")
    sl_wst[0].users.append(t_wob)
    t_vones = p.dma("sync", lambda e: e.dma_start(out=ones64[:, :], in_=vtones_in[:, :]),
                    inc="ld")

    for wname, wap in [("wq", wq_in), ("wk", wk_in), ("wv", wv_in), ("wo", wo_in)]:
        for km in range(KM):
            s = km % 2
            tl = p.dma("sync", lambda e, wap=wap, km=km, s=s: e.dma_start(
                out=wstage[s][:, :], in_=wap[km * 128:(km + 1) * 128, :]),
                waits=sl_wst[s].take(), inc=f"wst{s}")
            tc = p.op("vector", lambda e, s=s: e.tensor_copy(wcast[:, :], wstage[s][:, :]),
                      waits=[tl] + sl_wcast.take(), inc="vs")
            sl_wst[s].users.append(tc)
            td = p.dma("gpsimd", lambda e, wname=wname, km=km: e.dma_start(
                out=wscr[wname][km * 128:(km + 1) * 128, :], in_=wcast[:, :]),
                waits=[tc], inc="swr")
            sl_wcast.users.append(td)
            wscr_tok[wname][km] = td

    prev_add = None
    for km in range(KM):
        s = km % 2
        tl = p.dma("sync", lambda e, km=km, s=s: e.dma_start(
            out=wstage[s][:, :], in_=fw_in[km * 128:(km + 1) * 128, :]),
            waits=sl_wst[s].take(), inc=f"wst{s}")
        col = 6 if km == 0 else 7
        ta = p.op("scalar", lambda e, s=s, col=col: e.activation(
            absw[:, :], wstage[s][:, :], AF.Abs, accum_out=stt[:, col:col + 1]),
            waits=[tl] + ([prev_add] if km > 0 else []), inc="ss")
        sl_wst[s].users.append(ta)
        if km == 0:
            prev_add = p.op("scalar", lambda e: e.drain(), [ta], inc="ss")
        else:
            tadr = p.op("scalar", lambda e: e.drain(), [ta], inc="ss")
            prev_add = vsm(lambda e: e.tensor_tensor(
                stt[:, 6:7], stt[:, 6:7], stt[:, 7:8], ALU.add), [tadr])
    t_ws_tr = p.op("tensor", lambda e: e.transpose(bank[7][0:1, 0:128], stt[:, 6:7], id32[:, :]),
                   waits=[prev_add, t_id32] + sl_bank[7].take(), inc="ts")
    t_ws_cp = vsm(lambda e: e.tensor_copy(mx128[:, :], bank[7][0:1, 0:128]), [t_ws_tr])
    sl_bank[7].users.append(t_ws_cp)
    t_ws = vsm(lambda e: e.tensor_reduce(scal[0:1, 0:1], mx128[:, :], AX.X, ALU.add), [t_ws_cp])
    t_ws2 = vsm(lambda e: e.tensor_tensor(
        scal[0:1, 0:1], scal[0:1, 0:1], crow[0:1, 1:2], ALU.mult), [t_ws] + t_cc0)
    t_thr = vsm(lambda e: e.tensor_tensor(
        scal[0:1, 1:2], scal[0:1, 0:1], crow[0:1, 2:3], ALU.mult), [t_ws2])
    t_thrb = bcast_col(scal[0:1, 1:2], thrcol[:, 0:1], [t_thr])

    for km in range(KM):
        s = km % 2
        tl = p.dma("sync", lambda e, km=km, s=s: e.dma_start(
            out=wstage[s][:, :], in_=fw_in[km * 128:(km + 1) * 128, :]),
            waits=sl_wst[s].take(), inc=f"wst{s}")
        tabs = p.op("scalar", lambda e, s=s: e.activation(absw[:, :], wstage[s][:, :], AF.Abs),
                    waits=[tl] + sl_absw.take(), inc="ss")
        tsg = p.op("scalar", lambda e, s=s: e.activation(sgnw[:, :], wstage[s][:, :], AF.Sign),
                   waits=[tl] + sl_u2.take(), inc="ss")
        sl_wst[s].users += [tabs, tsg]
        tgt = p.op("vector", lambda e: e.tensor_scalar(
            gtw[:, :], absw[:, :], thrcol[:, 0:1], None, ALU.is_gt),
            waits=[tabs, t_thrb] + sl_u1.take(), inc="vs")
        sl_absw.users.append(tgt)
        tqw = p.op("vector", lambda e: e.tensor_tensor(qwc[:, :], sgnw[:, :], gtw[:, :], ALU.mult),
                   waits=[tgt, tsg] + sl_qwc.take(), inc="vs")
        sl_u1.users.append(tqw)
        sl_u2.users.append(tqw)
        bw = sl_bank[0].take()
        last = None
        for j in range(KM):
            last = p.op("tensor", lambda e, j=j: e.transpose(
                bank[0].bitcast(BF16)[:, j * 128:(j + 1) * 128],
                qwc[:, j * 128:(j + 1) * 128], idbf[:, :]),
                waits=([tqw, t_idbf] + bw) if j == 0 else [], inc="ts")
        sl_qwc.users.append(last)
        tev = p.op("vector", lambda e: e.tensor_copy(qwt_ev[:, :], bank[0].bitcast(BF16)[:, 0:1024]),
                   waits=[last] + sl_qwtev.take(), inc="vs")
        sl_bank[0].users.append(tev)
        dst = wscr["qwT"].rearrange("(j r) c -> r j c", r=128)[:, :, km * 128:(km + 1) * 128]
        src = qwt_ev[:, :].rearrange("p (j c) -> p j c", j=KM)
        tqd = p.dma("gpsimd", lambda e, dst=dst, src=src: e.dma_start(out=dst, in_=src),
                    waits=[tev], inc="swr")
        sl_bank[0].users.append(tqd)
        sl_qwtev.users.append(tqd)
        wscr_tok["qwT"][km] = tqd
        sl_r1.users.append(last)
        sl_r2.users.append(sl_wcast.users[-1] if sl_wcast.users else None)

    sl_qa[0].users += sl_u2.take()
    sl_c1.users += sl_absw.take()
    sl_qbf[0].users += sl_qwtev.take()
    for i_ in range(2):
        sl_xo[i_].users += sl_wst[i_].take()

    for seq in range(SEQS):
        for ttg in range(NTG):
            tls, slots = [], []
            for t4 in range(4):
                tt = ttg * 4 + t4
                idx = seq * NT + tt
                slot = idx % NXR
                tl = p.dma("sync", lambda e, seq=seq, tt=tt, slot=slot: e.dma_start(
                    out=xr[slot][:, :], in_=x_in[seq, tt * 128:(tt + 1) * 128, :]),
                    waits=sl_xr[slot].take(), inc=f"xin{slot}")
                tls.append([tl])
                slots.append(slot)
            base = seq * NT + ttg * 4
            (stok, sq_toks) = ln_stats4(
                [xr[s][:, :] for s in slots],
                st1m[:, base:base + 4], st1r[:, base:base + 4], tls)
            zts = []
            for t4 in range(4):
                idx = base + t4
                slot = slots[t4]
                sy = t4 % 2
                tz = p.op("vector", lambda e, slot=slot, idx=idx, sy=sy: e.tensor_scalar(
                    yb[sy][:, :], xr[slot][:, :], st1m[:, idx:idx + 1], st1r[:, idx:idx + 1],
                    ALU.subtract, ALU.mult), waits=stok + sl_y[sy].take(), inc="vs")
                sl_xr[slot].users += [tz, sq_toks[t4]]
                tm = p.op("vector", lambda e, t4=t4, sy=sy: e.tensor_reduce(
                    stt[:, 16 + t4:17 + t4], yb[sy][:, :], AX.X, ALU.max,
                    apply_absolute_value=True), waits=[tz], inc="vs")
                sl_y[sy].users.append(tm)
                zts.append(tm)
            trd = p.op("vector", lambda e: e.drain(), zts, inc="vs")
            tmx = vsm(lambda e: e.tensor_reduce(
                stt[:, 20:21], stt[:, 16:20], AX.X, ALU.max), [trd])
            last_macc[0] = vsm(lambda e: e.tensor_tensor(
                maxacc[:, 0:1], maxacc[:, 0:1], stt[:, 20:21], ALU.max),
                [tmx, t_mz] + ([last_macc[0]] if last_macc[0] else []))

    def allreduce_scale(maxcol, cc_idx, ri_dst, al_dst):
        t_drain = p.op("vector", lambda e: e.drain(), waits=[p.cur("vs")], inc="vs")
        t_pm = part_reduce_max(maxcol, scal[0:1, 2:3], [t_drain])
        t_bc = vsm(lambda e: e.tensor_copy(
            scal[0:1, 3:16], scal[0:1, 2:3].broadcast_to((1, 13))), [t_pm])
        t_ccw = p.dma("gpsimd", lambda e: e.dma_start(
            out=cci[cc_idx].ap()[:, :], in_=scal[0:1, 0:16]), waits=[t_bc, t_ws2], inc="swr")
        t_cc = p.op("gpsimd", lambda e: e.collective_compute(
            "AllReduce", ALU.max, replica_groups=[list(range(NCORES))],
            ins=[cci[cc_idx].ap().opt()], outs=[cco[cc_idx].ap().opt()]),
            waits=[t_ccw], inc="cc", dma=True)
        t_ccr = p.dma("gpsimd", lambda e: e.dma_start(
            out=ccres[0:1, cc_idx * 16:(cc_idx + 1) * 16], in_=cco[cc_idx].ap()[0:1, 0:16]),
            waits=[t_cc], inc=f"gld{cc_idx}")
        t_is = vsm(lambda e: e.tensor_tensor(
            scal[0:1, 4:5], ccres[0:1, cc_idx * 16 + 2:cc_idx * 16 + 3],
            crow[0:1, 3:4], ALU.mult),
            [t_ccr] + t_cc0)
        t_ri = newton_recip(scal[0:1, 4:5], scal[0:1, 5:6], [t_is])
        t_al = vsm(lambda e: e.tensor_tensor(
            scal[0:1, 6:7], scal[0:1, 0:1], scal[0:1, 4:5], ALU.mult), [t_is])
        t_alb = vsm(lambda e: e.tensor_tensor(
            scal[0:1, 6:7], scal[0:1, 6:7], crow[0:1, 2:3], ALU.mult), [t_al])
        t_wsis = vsm(lambda e: e.tensor_tensor(
            scal[0:1, 7:8], scal[0:1, 0:1], scal[0:1, 4:5], ALU.mult), [t_is])
        t_rb = newton_recip(scal[0:1, 7:8], scal[0:1, 8:9], [t_wsis])
        t_fbl = p.dma("gpsimd", lambda e: e.dma_start(
            out=wcast[0:1, :], in_=fb_in[:, :]), waits=sl_wcast.take(), inc=f"gfb{cc_idx}")
        t_brow_ = p.op("vector", lambda e: e.tensor_scalar(
            brow[:, :], wcast[0:1, :], scal[0:1, 8:9], None, ALU.mult),
            waits=[t_rb, t_fbl], inc="vs")
        t_brow = p.op("vector", lambda e: e.drain(), [t_brow_], inc="vs")
        sl_wcast.users.append(t_brow)
        t_ric = bcast_col(scal[0:1, 5:6], ri_dst, [t_ri])
        t_alc = bcast_col(scal[0:1, 6:7], al_dst, [t_alb])
        return t_ric, t_alc, t_brow

    t_ri1c, t_al1c, t_brow1 = allreduce_scale(maxacc[:, 0:1], 0, ricol[:, 0:1], alcol[:, 0:1])

    for seq in range(SEQS):
        z2toks = []

        def ffn1_post(tg, slots4, ev_toks, seq=seq):
            base = seq * NT + tg * 4
            stok, sq_toks = ln_stats4([xo[t4][:, :] for t4 in range(4)],
                                      l2m[:, 0:4], l2r[:, 0:4],
                                      [ev_toks[t4] for t4 in range(4)])
            for t4 in range(4):
                tt = tg * 4 + t4
                tz = p.op("vector", lambda e, t4=t4: e.tensor_scalar(
                    z2tm[:, :], xo[t4][:, :], l2m[:, t4:t4 + 1], l2r[:, t4:t4 + 1],
                    ALU.subtract, ALU.mult),
                    waits=stok + [sq_toks[t4]] + sl_z2tm.take(), inc="vs")
                dstz = z2[:, :].rearrange("p (k t) -> p k t", k=KM)[:, :, tt * 128:(tt + 1) * 128]
                tfm, tlast_tr = transpose_tile(z2tm[:, :], dstz,
                                               [tz] + (sl_z2.take() if tt == 0 else []),
                                               strided_dst=True, bank_i=t4)
                sl_z2tm.users += [tfm, tlast_tr]
                td = p.dma("gpsimd", lambda e, seq=seq, tt=tt, t4=t4: e.dma_start(
                    out=xp_scr[seq * S + tt * 128: seq * S + (tt + 1) * 128, :],
                    in_=xo[t4][:, :]), waits=ev_toks[t4] + [tz], inc=f"sxp{seq}")
                sl_xo[t4].users += [tz, sq_toks[t4], td]
                xp_w[(seq, tt)] = td
                z2toks.append(tfm)

        ffn_block_seq(seq,
                      lambda tt, seq=seq: (x_in[seq, tt * 128:(tt + 1) * 128, :], None),
                      lambda idx: st1m[:, idx:idx + 1], lambda idx: st1r[:, idx:idx + 1],
                      ricol[:, 0:1], t_ri1c, alcol[:, 0:1], t_al1c, t_brow1, ffn1_post)

        attn_toks = []
        avpar = 0
        ptc = 0
        attn_first = True
        for hp in range(KM):
            swk = 0
            srck = wscr["wk"].rearrange("(k r) c -> r k c", r=128)[:, :, hp * 128:(hp + 1) * 128]
            dstk = wr[swk][:, :].rearrange("p (k c) -> p k c", k=KM)
            twlk = p.dma("sync", lambda e, srck=srck, dstk=dstk: e.dma_start(
                out=dstk, in_=srck),
                waits=sl_wr[swk].take() + [wscr_tok["wk"][k] for k in range(KM)],
                inc=f"wld{swk}")
            mmk = [None] * NQB
            for km in range(KM):
                for t512 in range(NQB):
                    b = t512
                    bw = sl_bank[b].take() if km == 0 else []
                    mmk[b] = p.op("tensor", lambda e, km=km, swk=swk, t512=t512, b=b: e.matmul(
                        bank[b][:, :], wr[swk][:, km * 128:(km + 1) * 128],
                        z2[:, km * S + t512 * 512: km * S + (t512 + 1) * 512],
                        start=(km == 0), stop=(km == KM - 1)),
                        waits=[twlk] + z2toks + bw, inc="ts")
                    sl_wr[swk].users.append(mmk[b])
                    sl_z2.users.append(mmk[b])
            kev = []
            ktw = sl_ktmp.take()
            for t512 in range(NQB):
                te = p.op("vector", lambda e, t512=t512: e.tensor_copy(
                    ktmp[:, t512 * 512:(t512 + 1) * 512], bank[t512][:, :]),
                    waits=[mmk[t512]] + (ktw if t512 == 0 else []), inc="vs")
                sl_bank[t512].users.append(te)
                kev.append(te)
            kf = hp % 2
            kfw = sl_kfm[kf].take()
            tkf = None
            for sl_ in range(NQB):
                lo = sl_ * 512
                tk1 = p.op("gpsimd", lambda e, lo=lo: e.tensor_tensor(
                    kropet[:, :], ktmp[:, lo:lo + 512], costab[:, lo:lo + 512], ALU.mult),
                    waits=kev + [t_cos]
                    + ([tkf] if tkf else ([krope_prev[0]] if krope_prev[0] else []))
                    + (kfw if sl_ == 0 else []), inc="gs")
                tks = []
                for g in range(4):
                    gs = [1, 0, 3, 2][g]
                    tks.append(p.op("gpsimd", lambda e, g=g, gs=gs, lo=lo: e.tensor_tensor(
                        kropes[g * 32:(g + 1) * 32, :],
                        ktmp[gs * 32:(gs + 1) * 32, lo:lo + 512],
                        sintab[gs * 32:(gs + 1) * 32, lo:lo + 512], ALU.mult),
                        waits=[tk1, t_sin], inc="gs"))
                tkf = p.op("gpsimd", lambda e, kf=kf, lo=lo: e.tensor_tensor(
                    kfm[kf][:, lo:lo + 512], kropet[:, :], kropes[:, :],
                    ALU.add), waits=tks, inc="gs")
                sl_ktmp.users.append(tkf)
            krope_prev[0] = tkf
            krope_tok = tkf

            swv = 1
            srcv = wscr["wv"].rearrange("(k r) c -> r k c", r=128)[:, :, hp * 128:(hp + 1) * 128]
            dstv = wr[swv][:, :].rearrange("p (k c) -> p k c", k=KM)
            twlv = p.dma("sync", lambda e, srcv=srcv, dstv=dstv: e.dma_start(
                out=dstv, in_=srcv),
                waits=sl_wr[swv].take() + [wscr_tok["wv"][k] for k in range(KM)],
                inc=f"wld{swv}")
            vf = hp % 2
            vfw = sl_vp[vf].take()
            for t in range(NT):
                b = 4 + (t % 2)
                mmv = None
                bw = sl_bank[b].take()
                for km in range(KM):
                    mmv = p.op("tensor", lambda e, km=km, swv=swv, t=t, b=b: e.matmul(
                        bank[b][:, 0:128], z2[:, km * S + t * 128: km * S + (t + 1) * 128],
                        wr[swv][:, km * 128:(km + 1) * 128],
                        start=(km == 0), stop=(km == KM - 1)),
                        waits=[twlv] + z2toks + (bw if km == 0 else []), inc="ts")
                    sl_wr[swv].users.append(mmv)
                    sl_z2.users.append(mmv)
                tev = p.op("vector", lambda e, t=t, b=b, vf=vf: e.tensor_copy(
                    vpair[vf][:, t * 128:(t + 1) * 128], bank[b][:, 0:128]),
                    waits=[mmv] + (vfw if t == 0 else []), inc="vs")
                sl_bank[b].users.append(tev)
                sl_vp[vf].users.append(tev)
                if t == 0:
                    v_toks = []
                v_toks.append(tev)

            sw = 0
            srcw_ap = wscr["wq"].rearrange("(k r) c -> r k c", r=128)[:, :, hp * 128:(hp + 1) * 128]
            dstw_ap = wr[sw][:, :].rearrange("p (k c) -> p k c", k=KM)
            twl = p.dma("sync", lambda e, srcw_ap=srcw_ap, dstw_ap=dstw_ap: e.dma_start(
                out=dstw_ap, in_=srcw_ap),
                waits=sl_wr[sw].take() + [wscr_tok["wq"][k] for k in range(KM)], inc=f"wld{sw}")
            mm_last = [None] * NQB
            for km in range(KM):
                for t512 in range(NQB):
                    b = t512
                    bw = sl_bank[b].take() if km == 0 else []
                    mm_last[b] = p.op("tensor", lambda e, km=km, sw=sw, t512=t512, b=b: e.matmul(
                        bank[b][:, :], wr[sw][:, km * 128:(km + 1) * 128],
                        z2[:, km * S + t512 * 512: km * S + (t512 + 1) * 512],
                        start=(km == 0), stop=(km == KM - 1)),
                        waits=[twl] + z2toks + bw, inc="ts")
                    sl_wr[sw].users.append(mm_last[b])
                    sl_z2.users.append(mm_last[b])
            shp = hp % 2
            tqf = None
            qhw = sl_qhp[shp].take()
            for sl_ in range(NQB):
                lo = sl_ * 512
                tq1 = p.op("vector", lambda e, sl_=sl_, lo=lo: e.tensor_tensor(
                    ropet[:, :], bank[sl_][:, :], costab[:, lo:lo + 512], ALU.mult),
                    waits=[mm_last[sl_], t_cos]
                    + ([tqf] if tqf else ([prev_tqf[0]] if prev_tqf[0] else [])), inc="vs")
                tqs = []
                for g in range(4):
                    gsw = [1, 0, 3, 2][g]
                    tqs.append(p.op("vector", lambda e, sl_=sl_, g=g, gsw=gsw, lo=lo: e.tensor_tensor(
                        ropes[g * 32:(g + 1) * 32, :], bank[sl_][gsw * 32:(gsw + 1) * 32, :],
                        sintab[gsw * 32:(gsw + 1) * 32, lo:lo + 512], ALU.mult),
                        waits=[tq1, t_sin], inc="vs"))
                tqf = p.op("vector", lambda e, shp=shp, lo=lo: e.tensor_tensor(
                    qhp[shp][:, lo:lo + 512], ropet[:, :], ropes[:, :], ALU.add),
                    waits=tqs + (qhw if sl_ == 0 else []), inc="vs")
                sl_bank[sl_].users += [tqf]
            prev_tqf[0] = tqf

            for h2 in range(2):
                head = 2 * hp + h2
                for qb in range(NQB):
                    nkc = (qb + 1) * 4
                    avb = 6 if avpar == 0 else 2
                    azb = 7 if avpar == 0 else 3
                    avw = sl_bank[avb].take()
                    azw = sl_bank[azb].take()
                    av_mm = None
                    pend = []
                    for kc in range(nkc + 1):
                        if kc < nkc:
                            sb_ = 4 + (kc % 2)
                            sw_b = sl_bank[sb_].take()
                            tmm = p.op("tensor", lambda e, h2=h2, kc=kc, qb=qb, sb_=sb_, shp=shp, kf=kf: e.matmul(
                                bank[sb_][:, :],
                                kfm[kf][h2 * 64:(h2 + 1) * 64, kc * 128:(kc + 1) * 128],
                                qhp[shp][h2 * 64:(h2 + 1) * 64, qb * 512:(qb + 1) * 512],
                                start=True, stop=True),
                                waits=[tqf, krope_tok] + sw_b, inc="ts")
                            sl_qhp[shp].users.append(tmm)
                            sl_kfm[kf].users.append(tmm)
                            pt = ptc % 3
                            ptc += 1
                            texp = p.op("scalar", lambda e, sb_=sb_, pt=pt: e.activation(
                                ptile[pt][:, :], bank[sb_][:, :], AF.Exp,
                                bias=cshift[:, 0:1], scale=0.125),
                                waits=[tmm, t_csh] + sl_pt[pt].take(), inc="ss")
                            sl_bank[sb_].users.append(texp)
                            tlast = texp
                            j = kc - 4 * qb
                            if 0 <= j < 4:
                                tlast = p.op("vector", lambda e, pt=pt, j=j: e.tensor_tensor(
                                    ptile[pt][:, :], ptile[pt][:, :],
                                    masks[:, j * 512:(j + 1) * 512], ALU.mult),
                                    waits=[texp, t_mask], inc="vs")
                            pend.append((kc, pt, tlast))
                        if (kc >= 1 or kc == nkc) and pend:
                            kca, pt, tlast = pend.pop(0)
                            av_v = p.op("tensor", lambda e, kc=kca, h2=h2, pt=pt, avb=avb, nkc=nkc, vf=vf: e.matmul(
                                bank[avb][0:64, :],
                                vpair[vf][:, kc * 128 + h2 * 64: kc * 128 + (h2 + 1) * 64],
                                ptile[pt][:, :],
                                start=(kc == 0), stop=(kc == nkc - 1),
                                tile_position=(0, 0)),
                                waits=[tlast, v_toks[kca]] + (avw if kca == 0 else []),
                                inc="ts")
                            sl_vp[vf].users.append(av_v)
                            av_mm = p.op("tensor", lambda e, kc=kca, pt=pt, azb=azb, nkc=nkc: e.matmul(
                                bank[azb][64:128, :], ones64[:, :],
                                ptile[pt][:, :],
                                start=(kc == 0), stop=(kc == nkc - 1),
                                tile_position=(0, 64)),
                                waits=[av_v, t_vones] + (azw if kca == 0 else []), inc="ts")
                            sl_pt[pt].users.append(av_mm)
                    while pend:
                        raise RuntimeError("pipeline leftover")
                    rp = avpar
                    trc = p.op("vector", lambda e, azb=azb, rp=rp: e.reciprocal(
                        recb[rp], bank[azb][64:128, :]),
                        waits=[av_mm] + sl_rec[rp].take(), inc="vs")
                    tat = p.op("vector", lambda e, avb=avb, rp=rp, h2=h2, hp=hp, qb=qb: e.tensor_tensor(
                        attn[h2 * 64:(h2 + 1) * 64, hp * S + qb * 512: hp * S + (qb + 1) * 512],
                        bank[avb][0:64, :], recb[rp], ALU.mult),
                        waits=[trc] + (sl_attn.take() if attn_first else []), inc="vs")
                    attn_first = False
                    sl_bank[azb].users.append(trc)
                    sl_bank[avb].users.append(tat)
                    sl_rec[rp].users.append(tat)
                    attn_toks.append(tat)
                    avpar = 1 - avpar

        for ttg in range(NTG):
            xp_toks = []
            for t4 in range(4):
                tt = ttg * 4 + t4
                slot = tt % NXR
                tl = p.dma("sync", lambda e, seq=seq, tt=tt, slot=slot: e.dma_start(
                    out=xr[slot][:, :],
                    in_=xp_scr[seq * S + tt * 128: seq * S + (tt + 1) * 128, :]),
                    waits=sl_xr[slot].take() + [xp_w[(seq, tt)]], inc=f"xin{slot}")
                xp_toks.append((slot, tl))
            mm_last = [None] * 8
            for kf in range(KM):
                sw = kf % 2
                twl = p.dma("sync", lambda e, kf=kf, sw=sw: e.dma_start(
                    out=wr[sw][:, :], in_=wscr["wo"][kf * 128:(kf + 1) * 128, :]),
                    waits=sl_wr[sw].take() + [wscr_tok["wo"][kf]], inc=f"wld{sw}")
                for t4 in range(4):
                    for oh in range(2):
                        b = t4 * 2 + oh
                        tt = ttg * 4 + t4
                        bw = sl_bank[b].take() if kf == 0 else []
                        mm = p.op("tensor", lambda e, kf=kf, sw=sw, tt=tt, oh=oh, b=b: e.matmul(
                            bank[b][:, :], attn[:, kf * S + tt * 128: kf * S + (tt + 1) * 128],
                            wr[sw][:, oh * 512:(oh + 1) * 512],
                            start=(kf == 0), stop=False),
                            waits=[twl] + attn_toks + bw, inc="ts")
                        mm_last[b] = mm
                        sl_wr[sw].users.append(mm)
                        sl_attn.users.append(mm)
            for t4 in range(4):
                for oh in range(2):
                    b = t4 * 2 + oh
                    mm_last[b] = p.op("tensor", lambda e, oh=oh, b=b: e.matmul(
                        bank[b][:, :], onesb[:, :], wob_row[0:1, oh * 512:(oh + 1) * 512],
                        start=False, stop=True),
                        waits=[mm_last[b], t_wob, t_onesb], inc="ts")
            ev_toks = [[] for _ in range(4)]
            for t4 in range(4):
                tt = ttg * 4 + t4
                slot, tlx = xp_toks[t4]
                so = t4
                tw_xo = sl_xo[so].take()
                for oh in range(2):
                    b = t4 * 2 + oh
                    tev = p.op("vector", lambda e, b=b, oh=oh, slot=slot, so=so: e.tensor_tensor(
                        xo[so][:, oh * 512:(oh + 1) * 512], bank[b][:, :],
                        xr[slot][:, oh * 512:(oh + 1) * 512], ALU.add),
                        waits=[mm_last[b], tlx] + (tw_xo if oh == 0 else []), inc="vs")
                    sl_bank[b].users.append(tev)
                    ev_toks[t4].append(tev)
                sl_xr[slot].users += ev_toks[t4]
            base = seq * NT + ttg * 4
            stok, sq_toks = ln_stats4([xo[t4][:, :] for t4 in range(4)],
                                      st3m[:, base:base + 4], st3r[:, base:base + 4],
                                      [ev_toks[t4] for t4 in range(4)])
            zts = []
            for t4 in range(4):
                tt = ttg * 4 + t4
                idx = base + t4
                sy = t4 % 2
                tz = p.op("vector", lambda e, t4=t4, idx=idx, sy=sy: e.tensor_scalar(
                    yb[sy][:, :], xo[t4][:, :], st3m[:, idx:idx + 1], st3r[:, idx:idx + 1],
                    ALU.subtract, ALU.mult), waits=stok + sl_y[sy].take(), inc="vs")
                tm = p.op("vector", lambda e, t4=t4, sy=sy: e.tensor_reduce(
                    stt[:, 16 + t4:17 + t4], yb[sy][:, :], AX.X, ALU.max,
                    apply_absolute_value=True), waits=[tz], inc="vs")
                sl_y[sy].users.append(tm)
                zts.append(tm)
                td = p.dma("gpsimd", lambda e, seq=seq, tt=tt, t4=t4: e.dma_start(
                    out=xpp_scr[seq * S + tt * 128: seq * S + (tt + 1) * 128, :],
                    in_=xo[t4][:, :]), waits=ev_toks[t4] + [tz], inc=f"sxpp{seq}")
                sl_xo[t4].users += [tz, sq_toks[t4], td]
                xpp_w[(seq, tt)] = td
            trd = p.op("vector", lambda e: e.drain(), zts, inc="vs")
            tmx = vsm(lambda e: e.tensor_reduce(
                stt[:, 20:21], stt[:, 16:20], AX.X, ALU.max), [trd])
            last_macc[1] = vsm(lambda e: e.tensor_tensor(
                maxacc[:, 1:2], maxacc[:, 1:2], stt[:, 20:21], ALU.max),
                [tmx] + ([last_macc[1]] if last_macc[1] else []))

    t_ri2c, t_al2c, t_brow2 = allreduce_scale(maxacc[:, 1:2], 1, ricol[:, 1:2], alcol[:, 1:2])

    for seq in range(SEQS):
        def ffn2_post(tg, slots4, ev_toks, seq=seq):
            for t4 in range(4):
                tt = tg * 4 + t4
                td = p.dma("gpsimd", lambda e, seq=seq, tt=tt, t4=t4: e.dma_start(
                    out=out[seq, tt * 128:(tt + 1) * 128, :], in_=xo[t4][:, :]),
                    waits=ev_toks[t4], inc="sout")
                sl_xo[t4].users.append(td)

        ffn_block_seq(seq,
                      lambda tt, seq=seq: (
                          xpp_scr[seq * S + tt * 128: seq * S + (tt + 1) * 128, :],
                          xpp_w[(seq, tt)]),
                      lambda idx: st3m[:, idx:idx + 1], lambda idx: st3r[:, idx:idx + 1],
                      ricol[:, 1:2], t_ri2c, alcol[:, 1:2], t_al2c, t_brow2, ffn2_post)

    if DBG:
        dbg = {}
        for nm, sh in [("dbg_xp", [SEQS * S, D]), ("dbg_xpp", [SEQS * S, D])]:
            dbg[nm] = nc.dram_tensor(nm, sh, F32, kind="ExternalOutput").ap()
        STW = [p.cur(s_) for s_ in list(p.counts)
               if s_.startswith(("sout", "sxp", "swr", "gld", "gfb"))]
        for nm, t, sh in [("dbg_scal", scal, [1, 16]), ("dbg_ccres", ccres, [1, 32]),
                          ("dbg_st1m", st1m, [128, SEQS * NT]),
                          ("dbg_st1r", st1r, [128, SEQS * NT]),
                          ("dbg_st3m", st3m, [128, SEQS * NT]),
                          ("dbg_st3r", st3r, [128, SEQS * NT]),
                          ("dbg_ricol", ricol, [128, 2]), ("dbg_alcol", alcol, [128, 2]),
                          ("dbg_maxacc", maxacc, [128, 2])]:
            a = nc.dram_tensor(nm, sh, F32, kind="ExternalOutput").ap()
            p.dma("gpsimd", lambda e, a=a, t=t: e.dma_start(out=a[:, :], in_=t[:, :]),
                  waits=STW + [p.cur("vs"), p.cur("gs"), p.cur("ts")], inc="swr")
        for nm, t, sh in [("dbg_z2", z2, [128, KM * S]),
                          ("dbg_attn", attn, [128, KM * S]),
                          ("dbg_q7", qhp[1], [128, S])]:
            a = nc.dram_tensor(nm, sh, BF16, kind="ExternalOutput").ap()
            p.dma("gpsimd", lambda e, a=a, t=t: e.dma_start(out=a[:, :], in_=t[:, :]),
                  waits=STW + [p.cur("vs"), p.cur("gs"), p.cur("ts")], inc="swr")
        qwt_dbg = nc.dram_tensor("dbg_qwt", [D, D], BF16, kind="ExternalOutput").ap()
        for i in range(KM):
            tl = p.dma("sync", lambda e, i=i: e.dma_start(
                out=wr[0][:, :], in_=wscr["qwT"][i * 128:(i + 1) * 128, :]),
                waits=STW + [p.cur("ts")] + sl_wr[0].take(), inc="wld0")
            td = p.dma("gpsimd", lambda e, i=i: e.dma_start(
                out=qwt_dbg[i * 128:(i + 1) * 128, :], in_=wr[0][:, :]),
                waits=[tl], inc="swr")
            sl_wr[0].users.append(td)
        for nm, srct in [("dbg_xp", xp_scr), ("dbg_xpp", xpp_scr)]:
            for i in range(SEQS * NT):
                tl = p.dma("sync", lambda e, srct=srct, i=i: e.dma_start(
                    out=xr[0][:, :], in_=srct[i * 128:(i + 1) * 128, :]),
                    waits=STW + [p.cur("vs")] + sl_xr[0].take(), inc="xin0")
                td = p.dma("gpsimd", lambda e, nm=nm, i=i: e.dma_start(
                    out=dbg[nm][i * 128:(i + 1) * 128, :], in_=xr[0][:, :]),
                    waits=[tl], inc="swr")
                sl_xr[0].users.append(td)

    p.op("gpsimd", lambda e: e.nop(), waits=[p.cur(s_) for s_ in list(p.counts) if s_.startswith(("sout", "sxp", "swr", "gld", "gfb"))])

    sems = {}
    for s_ in p.counts:
        sems[s_] = es.enter_context(nc.semaphore(s_))
    with nc.Block() as block:
        @block.sync
        def _(eng):
            p.emit("sync", eng, sems)

        @block.vector
        def _(eng):
            p.emit("vector", eng, sems)

        @block.scalar
        def _(eng):
            p.emit("scalar", eng, sems)

        @block.tensor
        def _(eng):
            p.emit("tensor", eng, sems)

        @block.gpsimd
        def _(eng):
            p.emit("gpsimd", eng, sems)

    es.close()
    return nc


def kernel(L0_attn_x, L0_attn_wq, L0_attn_wk, L0_attn_wv, L0_attn_wo_w,
           L0_attn_wo_b, L0_ffn_weight, L0_ffn_bias):
    B, S, D_ = L0_attn_x.shape
    assert D_ == D and B % NCORES == 0
    SEQS = B // NCORES
    x = np.ascontiguousarray(np.asarray(L0_attn_x, dtype=np.float32))
    perm = rope_perm()
    wq = np.ascontiguousarray(np.asarray(L0_attn_wq, np.float32)[:, perm])
    wk = np.ascontiguousarray(np.asarray(L0_attn_wk, np.float32)[:, perm])
    wv = np.ascontiguousarray(np.asarray(L0_attn_wv, np.float32))
    wo = np.ascontiguousarray(np.asarray(L0_attn_wo_w, np.float32))
    wob = np.ascontiguousarray(np.asarray(L0_attn_wo_b, np.float32)).reshape(1, D)
    fw = np.ascontiguousarray(np.asarray(L0_ffn_weight, np.float32))
    fb = np.ascontiguousarray(np.asarray(L0_ffn_bias, np.float32)).reshape(1, D)
    costab, sintab, masks = host_tables(S)
    id32 = np.eye(128, dtype=np.float32)
    idbf = np.eye(128, dtype=ml_dtypes.bfloat16)
    ccol_c = np.zeros((128, 32), np.float32)
    ccol_c[:, 16:20] = 1.0 / D
    ccol_c[:, 20:24] = EPS
    ccol_c[:, 24:28] = 3.0
    ccol_c[:, 28:32] = -0.5
    ccol_c[:, 0] = 1.0 / D
    ccol_c[:, 1] = EPS
    ccol_c[:, 2] = 3.0
    ccol_c[:, 3] = -0.5
    ccol_c[:, 4] = C_SHIFT
    ccol_c[:, 5] = -0.5
    crow_c = np.zeros((1, 16), np.float32)
    crow_c[0, :5] = [2.0, 1.0 / (D * D), 0.5, 1.0 / 127.0, -1.0]
    onesf_c = np.ones((1, 128), np.float32)
    onesb_c = np.ones((1, 128), ml_dtypes.bfloat16)
    vtones_c = np.ones((128, 64), ml_dtypes.bfloat16)

    nc = build_nc(SEQS, S)
    in_maps = []
    for c in range(NCORES):
        in_maps.append({
            "x": x[c * SEQS:(c + 1) * SEQS],
            "wq": wq, "wk": wk, "wv": wv, "wo": wo,
            "wob": wob, "fw": fw, "fb": fb,
            "id32": id32, "idbf": idbf,
            "costab": costab, "sintab": sintab, "masks": masks,
            "ccol_c": ccol_c, "crow_c": crow_c, "onesf_c": onesf_c,
            "onesb_c": onesb_c, "vtones_c": vtones_c,
        })
    res = run_bass_kernel_spmd(nc, in_maps, core_ids=list(range(NCORES)))
    if DBG:
        return res
    return np.concatenate([r["out"] for r in res.results], axis=0)


# version 12
# speedup vs baseline: 1.5920x; 1.1978x over previous
import numpy as np
import ml_dtypes
from contextlib import ExitStack
from dataclasses import replace as dc_replace

import concourse.bass as bass
import concourse.mybir as mybir
from concourse.bass_utils import run_bass_kernel_spmd

F32 = mybir.dt.float32
BF16 = mybir.dt.bfloat16
I32 = mybir.dt.int32
AF = mybir.ActivationFunctionType
ALU = mybir.AluOpType
AX = mybir.AxisListType

NCORES = 8
D = 1024
H = 16
HD = 64
KM = D // 128
EPS = 1e-5
C_SHIFT = -8.0
DBG = False
ALIAS_ROPE = True
ALIAS_W = True


class Prog:

    def __init__(self):
        self.ops = []
        self.counts = {}
        self.history = {}

    def sem(self, name):
        self.counts.setdefault(name, 0)
        self.history.setdefault(name, [])

    def op(self, eng, fn, waits=(), inc=None, amt=1, dma=False):
        tok = None
        if inc is not None:
            self.sem(inc)
        if inc is not None:
            self.counts[inc] += amt
            self.history[inc].append((self.counts[inc], eng, dma))
            tok = (inc, self.counts[inc])
        ws = []
        for w in waits:
            if w is None:
                continue
            if isinstance(w, list):
                ws += [x for x in w if x is not None]
            else:
                ws.append(w)
        self.ops.append((eng, ws, (inc, amt) if inc else None, fn))
        return tok

    def dma(self, eng, fn, waits=(), inc="swr"):
        return self.op(eng, fn, waits, inc, amt=16, dma=True)

    def cur(self, sem):
        return (sem, self.counts[sem])

    def emit(self, engname, eng, sems):
        dma_sems = {s for s, hist in self.history.items()
                    if any(isdma for (_, _, isdma) in hist)}
        watermark = {}
        prefix = {s: 0 for s in self.counts}
        for (e, waits, inc, fn) in self.ops:
            if e == engname:
                merged = {}
                for s, v in waits:
                    if s in dma_sems:
                        v = prefix[s]
                    if v > merged.get(s, 0):
                        merged[s] = v
                for s, v in sorted(merged.items()):
                    if watermark.get(s, 0) >= v:
                        continue
                    eng.wait_ge(sems[s], v)
                    watermark[s] = v
                ins = fn(eng)
                if inc is not None:
                    ins.then_inc(sems[inc[0]], inc[1])
            if inc is not None:
                prefix[inc[0]] += inc[1]


class Slot:
    def __init__(self):
        self.users = []

    def take(self):
        u, self.users = self.users, []
        return u


def host_tables(S):
    half = HD // 2
    inv = 1.0 / (10000.0 ** (np.arange(0, HD, 2)[:half].astype(np.float64) / HD))
    ang = np.outer(np.arange(S, dtype=np.float64), inv)
    cos = np.cos(ang).astype(np.float32).T
    sin = np.sin(ang).astype(np.float32).T
    costab = np.empty((128, S), np.float32)
    sintab = np.empty((128, S), np.float32)
    for g in range(4):
        costab[g * 32:(g + 1) * 32] = cos
        sintab[g * 32:(g + 1) * 32] = sin * (1.0 if g % 2 == 0 else -1.0)
    masks = np.empty((128, 2048), np.float32)
    kk = np.arange(128)[:, None]
    qq = np.arange(512)[None, :]
    for j in range(4):
        masks[:, j * 512:(j + 1) * 512] = (qq >= kk + j * 128)
    bf = ml_dtypes.bfloat16
    return costab.astype(bf), sintab.astype(bf), masks.astype(bf)


def rope_perm():
    pm = []
    for h in range(H):
        base = h * HD
        pm += [base + i for i in range(0, HD, 2)]
        pm += [base + i for i in range(1, HD, 2)]
    return np.array(pm, np.int64)


def vt_head_ap(vt_t, head):
    full = vt_t[:, :]
    blockstep = (H * HD) - head * HD
    new_dims = [list(full.ap[0]), [blockstep, 2], [1, HD]]
    return dc_replace(full, offset=full.offset + head * HD, ap=new_dims)


def build_nc(SEQS, S):
    NT = S // 128
    NTG = NT // 4
    NQB = S // 512

    nc = bass.Bass("TRN2", target_bir_lowering=False, debug=False)

    x_in = nc.dram_tensor("x", [SEQS, S, D], F32, kind="ExternalInput").ap()
    wq_in = nc.dram_tensor("wq", [D, D], F32, kind="ExternalInput").ap()
    wk_in = nc.dram_tensor("wk", [D, D], F32, kind="ExternalInput").ap()
    wv_in = nc.dram_tensor("wv", [D, D], F32, kind="ExternalInput").ap()
    wo_in = nc.dram_tensor("wo", [D, D], F32, kind="ExternalInput").ap()
    wob_in = nc.dram_tensor("wob", [1, D], F32, kind="ExternalInput").ap()
    fw_in = nc.dram_tensor("fw", [D, D], F32, kind="ExternalInput").ap()
    fb_in = nc.dram_tensor("fb", [1, D], F32, kind="ExternalInput").ap()
    id32_in = nc.dram_tensor("id32", [128, 128], F32, kind="ExternalInput").ap()
    idbf_in = nc.dram_tensor("idbf", [128, 128], BF16, kind="ExternalInput").ap()
    cos_in = nc.dram_tensor("costab", [128, S], BF16, kind="ExternalInput").ap()
    sin_in = nc.dram_tensor("sintab", [128, S], BF16, kind="ExternalInput").ap()
    mask_in = nc.dram_tensor("masks", [128, 2048], BF16, kind="ExternalInput").ap()
    ccol_in = nc.dram_tensor("ccol_c", [128, 32], F32, kind="ExternalInput").ap()
    crow_in = nc.dram_tensor("crow_c", [1, 16], F32, kind="ExternalInput").ap()
    onesf_in = nc.dram_tensor("onesf_c", [1, 128], F32, kind="ExternalInput").ap()
    onesb_in = nc.dram_tensor("onesb_c", [1, 128], BF16, kind="ExternalInput").ap()
    vtones_in = nc.dram_tensor("vtones_c", [128, 64], BF16, kind="ExternalInput").ap()
    out = nc.dram_tensor("out", [SEQS, S, D], F32, kind="ExternalOutput").ap()

    xp_scr = nc.dram_tensor("xp_scr", [SEQS * S, D], F32).ap()
    xpp_scr = nc.dram_tensor("xpp_scr", [SEQS * S, D], F32).ap()
    wscr = {w: nc.dram_tensor(f"{w}_scr", [D, D], BF16).ap()
            for w in ["wq", "wk", "wv", "wo", "qwT"]}
    cci = [nc.dram_tensor(f"cc{i}_in", [1, 16], F32) for i in range(2)]
    cco = [nc.dram_tensor(f"cc{i}_out", [1, 16], F32) for i in range(2)]

    es = ExitStack()
    sbuf = lambda n, sh, dt: es.enter_context(nc.sbuf_tensor(n, sh, dt))
    psumt = lambda n, sh, dt: es.enter_context(nc.psum_tensor(n, sh, dt))

    id32 = sbuf("c_id32", [128, 128], F32)
    idbf = sbuf("c_idbf", [128, 128], BF16)
    costab = sbuf("c_cos", [128, S], BF16)
    sintab = sbuf("c_sin", [128, S], BF16)
    masks = sbuf("c_masks", [128, 2048], BF16)
    onesf = sbuf("c_onesf", [1, 128], F32)
    onesb = sbuf("c_onesb", [1, 128], BF16)
    wob_row = sbuf("c_wobrow", [1, D], BF16)
    brow = sbuf("c_brow", [1, D], BF16)

    NXR = 4
    xr = [sbuf(f"b_xr{i}", [128, D], F32) for i in range(NXR)]
    xo = [sbuf(f"b_xo{i}", [128, D], F32) for i in range(4)]
    yb = [sbuf(f"b_y{i}", [128, D], F32) for i in range(2)]
    ib0 = sbuf("b_i0", [128, D], I32)
    ib = [ib0, ib0]
    qab0 = sbuf("b_qa0", [128, D], BF16)
    qab = [qab0, qab0]
    c1b = sbuf("b_c1", [128, D], F32)
    c2b = c1b
    qbf = [sbuf(f"b_qbf{i}", [128, D], BF16) for i in range(2)]
    qfm = [[sbuf(f"b_qfm{g}_{i}", [128, D], BF16) for i in range(4)] for g in range(2)]

    st1m = sbuf("s_st1m", [128, SEQS * NT], F32)
    st1r = sbuf("s_st1r", [128, SEQS * NT], F32)
    st3m = sbuf("s_st3m", [128, SEQS * NT], F32)
    st3r = sbuf("s_st3r", [128, SEQS * NT], F32)
    ricol = sbuf("s_ricol", [128, 2], F32)
    alcol = sbuf("s_alcol", [128, 2], F32)
    thrcol = sbuf("s_thrcol", [128, 1], F32)
    acol = sbuf("s_acol", [128, 2], F32)
    l2m = sbuf("s_l2m", [128, 4], F32)
    l2r = sbuf("s_l2r", [128, 4], F32)
    scal = sbuf("s_scal", [1, 16], F32)
    ccres = sbuf("s_ccres", [1, 32], F32)
    stt = sbuf("s_stt", [128, 24], F32)
    sqs = sbuf("s_sqs", [128, D], BF16)
    z2tm = sqs
    maxacc = sbuf("s_maxacc", [128, 2], F32)
    cshift = sbuf("s_cshift", [128, 1], F32)
    chalf = sbuf("s_chalf", [128, 1], F32)
    ccol = sbuf("s_ccol", [128, 32], F32)
    crow = sbuf("s_crow", [1, 16], F32)
    mx128 = sbuf("s_mx128", [1, 128], F32)

    z2 = sbuf("a_z2", [128, KM * S], BF16)
    kfm = [sbuf(f"a_kfm{i}", [128, S], BF16) for i in range(2)]
    vpair = [sbuf(f"a_vp{i}", [128, (S // 128) * 192], BF16) for i in range(2)]
    attn = sbuf("a_attn", [128, KM * S], BF16)
    ktmp = sbuf("a_ktmp", [128, S], BF16)
    qhp = [sbuf(f"a_qhp{i}", [128, S], BF16) for i in range(2)]
    ropet = sbuf("a_ropet", [128, 512], BF16)
    ropes = sbuf("a_ropes", [128, 512], BF16)
    kropet = sbuf("a_kropet", [128, 512], BF16)
    kropes = sbuf("a_kropes", [128, 512], BF16)
    ptile = [sbuf(f"a_pt{i}", [128, 512], BF16) for i in range(3)]
    recb2 = sbuf("a_rec", [128, 512], F32)
    recb = [recb2[0:64, :], recb2[64:128, :]]

    wr = [sbuf(f"w_r{i}", [128, D], BF16) for i in range(2)]
    wstage = xo[0:2]
    absw = c1b
    wcast = sbuf("w_cast", [128, D], BF16)
    sgnw = qab0
    gtw = sbuf("w_gtw", [128, D], BF16)
    qwc = sbuf("w_qwc", [128, D], BF16)
    qwt_ev = qbf[0]

    bank = [psumt(f"p_b{i}", [128, 512], F32) for i in range(8)]

    p = Prog()
    for s_ in ["ld", "ts", "vs", "ss", "gs", "cc", "sout", "swr", "gld"]:
        p.sem(s_)

    last_macc = [None, None]
    prev_tqf = [None]
    krope_prev = [None]
    sl_xr = [Slot() for _ in range(NXR)]
    sl_xo = [Slot() for _ in range(4)]
    sl_y = [Slot() for _ in range(2)]
    _sl_i0 = Slot()
    sl_i = [_sl_i0, _sl_i0]
    _sl_qa0 = Slot()
    sl_qa = [_sl_qa0, _sl_qa0]
    sl_qbf = [Slot() for _ in range(2)]
    sl_qfm = [[Slot() for _ in range(4)] for _ in range(2)]
    sl_wr = [Slot() for _ in range(2)]
    sl_wst = [Slot() for _ in range(2)]
    sl_wcast = Slot()
    sl_bank = [Slot() for _ in range(8)]
    sl_pt = [Slot() for _ in range(3)]
    sl_rec = [Slot() for _ in range(2)]
    sl_qhp = [Slot() for _ in range(2)]
    sl_c1 = Slot()
    sl_c2 = sl_c1
    sl_u1 = Slot()
    sl_u2 = Slot()
    sl_r1 = Slot()
    sl_r2 = Slot()
    sl_absw = Slot()
    sl_qwc = Slot()
    sl_qwtev = Slot()
    sl_rope = Slot()
    sl_sttq = Slot()
    sl_stt4 = Slot()
    sl_z2 = Slot()
    sl_kfm = [Slot() for _ in range(2)]
    sl_attn = Slot()
    sl_ktmp = Slot()
    sl_vp = [Slot() for _ in range(2)]
    sl_z2tm = Slot()
    wscr_tok = {k: {} for k in wscr}
    xp_w = {}
    xpp_w = {}

    def vsm(fn, waits):
        t_ = p.op("vector", fn, waits, inc="vs")
        return p.op("vector", lambda e: e.drain(), [t_], inc="vs")

    def ssm(fn, waits):
        t_ = p.op("scalar", fn, waits, inc="ss")
        return p.op("scalar", lambda e: e.drain(), [t_], inc="ss")

    def bcast_col(src11, dstcol, waits):
        bw = sl_bank[7].take()
        t1 = p.op("tensor", lambda e: e.matmul(bank[7][:, 0:1], onesf[:, :], src11,
                                               start=True, stop=True),
                  waits=list(waits) + bw, inc="ts")
        t2 = vsm(lambda e: e.tensor_copy(dstcol, bank[7][:, 0:1]), [t1])
        sl_bank[7].users.append(t2)
        return t2

    def part_reduce_max(srccol, dst11, waits):
        bw = sl_bank[7].take()
        t1 = p.op("tensor", lambda e: e.transpose(bank[7][0:1, 0:128], srccol, id32[:, :]),
                  waits=list(waits) + bw, inc="ts")
        t2 = vsm(lambda e: e.tensor_copy(mx128[:, :], bank[7][0:1, 0:128]), [t1])
        sl_bank[7].users.append(t2)
        t3 = vsm(lambda e: e.tensor_reduce(dst11, mx128[:, :], AX.X, ALU.max), [t2])
        return t3

    def ln_stats(xap, mcol_ap, rcol_ap, waits_x):
        tv0 = vsm(lambda e: e.tensor_reduce(stt[:, 0:1], xap, AX.X, ALU.add), waits_x)
        ts0_ = p.op("scalar", lambda e: e.activation(sqs[:, :], xap, AF.Square,
                                                     accum_out=stt[:, 1:2]),
                    waits=list(waits_x) + sl_sttq.take() + sl_z2tm.take(), inc="ss")
        ts0 = p.op("scalar", lambda e: e.drain(), [ts0_], inc="ss")
        t1 = vsm(lambda e: e.tensor_tensor(
            mcol_ap, stt[:, 0:1], ccol[:, 0:1], ALU.mult), [tv0] + t_cc0)
        t2 = vsm(lambda e: e.tensor_tensor(
            stt[:, 2:3], mcol_ap, mcol_ap, ALU.mult), [t1])
        t3 = vsm(lambda e: e.tensor_tensor(
            stt[:, 3:4], stt[:, 1:2], ccol[:, 0:1], ALU.mult), [ts0])
        sl_sttq.users.append(t3)
        t4 = vsm(lambda e: e.tensor_tensor(
            stt[:, 3:4], stt[:, 3:4], stt[:, 2:3], ALU.subtract), [t2, t3])
        t5 = vsm(lambda e: e.tensor_tensor(
            stt[:, 3:4], stt[:, 3:4], ccol[:, 1:2], ALU.add), [t4])
        t6a = ssm(lambda e: e.activation(stt[:, 5:6], stt[:, 3:4], AF.Sqrt),
                  [t5] + sl_stt4.take())
        t6 = vsm(lambda e: e.reciprocal(stt[:, 4:5], stt[:, 5:6]), [t6a])
        t7 = vsm(lambda e: e.tensor_tensor(
            stt[:, 5:6], stt[:, 4:5], stt[:, 4:5], ALU.mult), [t6])
        t8 = vsm(lambda e: e.tensor_tensor(
            stt[:, 5:6], stt[:, 5:6], stt[:, 3:4], ALU.mult), [t7])
        t9a = vsm(lambda e: e.tensor_tensor(
            stt[:, 5:6], stt[:, 5:6], ccol[:, 2:3], ALU.subtract), [t8])
        t9 = vsm(lambda e: e.tensor_tensor(
            stt[:, 5:6], stt[:, 5:6], ccol[:, 3:4], ALU.mult), [t9a])
        t10 = vsm(lambda e: e.tensor_tensor(
            rcol_ap, stt[:, 5:6], stt[:, 4:5], ALU.mult), [t9])
        sl_stt4.users.append(t10)
        return t10, ts0

    def newton_recip(src11, dst11, waits):
        t1 = vsm(lambda e: e.reciprocal(dst11, src11), waits)
        t2 = vsm(lambda e: e.tensor_tensor(scal[0:1, 14:15], dst11, src11, ALU.mult), [t1])
        t3 = vsm(lambda e: e.tensor_tensor(
            scal[0:1, 14:15], crow[0:1, 0:1], scal[0:1, 14:15], ALU.subtract),
            [t2] + t_cc0)
        t4 = vsm(lambda e: e.tensor_tensor(dst11, dst11, scal[0:1, 14:15], ALU.mult), [t3])
        return t4

    def ln_stats4(xaps, mdest, rdest, waits_each):
        rts = []
        for t4 in range(4):
            rts.append(p.op("vector", lambda e, t4=t4: e.tensor_reduce(
                stt[:, t4:t4 + 1], xaps[t4], AX.X, ALU.add),
                waits=waits_each[t4], inc="vs"))
        trd = p.op("vector", lambda e: e.drain(), rts, inc="vs")
        sqs_toks = []
        for t4 in range(4):
            sqs_toks.append(p.op("scalar", lambda e, t4=t4: e.activation(
                sqs[:, :], xaps[t4], AF.Square, accum_out=stt[:, 4 + t4:5 + t4]),
                waits=list(waits_each[t4]) + sl_sttq.take() + sl_z2tm.take()
                + sqs_toks[-1:], inc="ss"))
        tsd = p.op("scalar", lambda e: e.drain(), sqs_toks, inc="ss")
        m4 = vsm(lambda e: e.tensor_tensor(
            stt[:, 8:12], stt[:, 0:4], ccol[:, 16:20], ALU.mult), [trd] + t_cc0)
        mc = vsm(lambda e: e.tensor_copy(mdest, stt[:, 8:12]), [m4])
        ex2 = vsm(lambda e: e.tensor_tensor(
            stt[:, 4:8], stt[:, 4:8], ccol[:, 16:20], ALU.mult), [tsd])
        sl_sttq.users.append(ex2)
        msq = vsm(lambda e: e.tensor_tensor(
            stt[:, 12:16], stt[:, 8:12], stt[:, 8:12], ALU.mult), [m4])
        ve0 = vsm(lambda e: e.tensor_tensor(
            stt[:, 4:8], stt[:, 4:8], stt[:, 12:16], ALU.subtract), [ex2, msq])
        ve = vsm(lambda e: e.tensor_tensor(
            stt[:, 4:8], stt[:, 4:8], ccol[:, 20:24], ALU.add), [ve0])
        sq4 = ssm(lambda e: e.activation(stt[:, 12:16], stt[:, 4:8], AF.Sqrt),
                  [ve] + sl_stt4.take())
        r0 = vsm(lambda e: e.reciprocal(stt[:, 8:12], stt[:, 12:16]), [sq4])
        n1 = vsm(lambda e: e.tensor_tensor(
            stt[:, 12:16], stt[:, 8:12], stt[:, 8:12], ALU.mult), [r0])
        n2 = vsm(lambda e: e.tensor_tensor(
            stt[:, 12:16], stt[:, 12:16], stt[:, 4:8], ALU.mult), [n1])
        n3 = vsm(lambda e: e.tensor_tensor(
            stt[:, 12:16], stt[:, 12:16], ccol[:, 24:28], ALU.subtract), [n2])
        n4 = vsm(lambda e: e.tensor_tensor(
            stt[:, 12:16], stt[:, 12:16], ccol[:, 28:32], ALU.mult), [n3])
        rr = vsm(lambda e: e.tensor_tensor(
            rdest, stt[:, 12:16], stt[:, 8:12], ALU.mult), [n4])
        sl_stt4.users.append(rr)
        return ([mc, rr], sqs_toks)

    def transpose_tile(src_ap, dst_ap, waits, strided_dst=False, bank_i=0):
        bw = sl_bank[bank_i].take()
        last = None
        for j in range(KM):
            last = p.op("tensor", lambda e, j=j: e.transpose(
                bank[bank_i].bitcast(BF16)[:, j * 128:(j + 1) * 128],
                src_ap[:, j * 128:(j + 1) * 128], idbf[:, :]),
                waits=(list(waits) + bw + [t_idbf]) if j == 0 else [], inc="ts")
        if strided_dst:
            src_ps = bank[bank_i].bitcast(BF16)[:, 0:1024].rearrange("p (j c) -> p j c", j=KM)
        else:
            src_ps = bank[bank_i].bitcast(BF16)[:, 0:1024]
        tev = p.op("vector", lambda e: e.tensor_copy(dst_ap, src_ps),
                   waits=[last], inc="vs")
        sl_bank[bank_i].users.append(tev)
        return tev, last

    def quant_chain(src_fn, m_ap, r_ap, ri_ap, src_users, extra_waits, sy):
        ta_ = p.op("vector", lambda e: e.tensor_tensor(
            acol[:, 0:1], r_ap, ri_ap, ALU.mult), waits=extra_waits, inc="vs")
        ta = p.op("vector", lambda e: e.drain(), waits=[ta_], inc="vs")
        ty = p.op("vector", lambda e: e.tensor_scalar(
            yb[sy][:, :], src_fn(), m_ap, acol[:, 0:1],
            ALU.subtract, ALU.mult), waits=[ta] + sl_y[sy].take(), inc="vs")
        src_users.append(ty)
        ti = p.op("vector", lambda e: e.tensor_copy(ib[sy][:, :], yb[sy][:, :]),
                  waits=[ti_w for ti_w in [ty]] + sl_i[sy].take(), inc="vs")
        tqa = p.op("vector", lambda e: e.tensor_copy(qab[sy][:, :], ib[sy][:, :]),
                   waits=[ti] + sl_qa[sy].take(), inc="vs")
        td_ = p.op("gpsimd", lambda e: e.tensor_tensor(
            c2b[:, :], yb[sy][:, :], ib[sy][:, :], ALU.subtract),
            waits=[ti] + sl_c2.take(), inc="gs")
        sl_y[sy].users.append(td_)
        sl_i[sy].users.append(td_)
        tu1 = p.op("scalar", lambda e: e.activation(
            gtw[:, :], c2b[:, :], AF.Sign, bias=chalf[:, 0:1], scale=1.0),
            waits=[td_, t_chf] + sl_u1.take(), inc="ss")
        tr1 = p.op("scalar", lambda e: e.activation(qwc[:, :], gtw[:, :], AF.Relu),
                   waits=[tu1] + sl_r1.take(), inc="ss")
        sl_u1.users.append(tr1)
        tu2 = p.op("scalar", lambda e: e.activation(
            gtw[:, :], c2b[:, :], AF.Sign, bias=chalf[:, 0:1], scale=-1.0),
            waits=[td_, tr1], inc="ss")
        sl_c2.users += [tu1, tu2]
        tr2 = p.op("scalar", lambda e: e.activation(wcast[:, :], gtw[:, :], AF.Relu),
                   waits=[tu2] + sl_r2.take(), inc="ss")
        sl_u1.users.append(tr2)
        te = p.op("gpsimd", lambda e: e.tensor_tensor(
            c1b[:, :], qwc[:, :], wcast[:, :], ALU.subtract),
            waits=[tr1, tr2] + sl_c1.take(), inc="gs")
        sl_r1.users.append(te)
        sl_r2.users.append(te)
        tq = p.op("vector", lambda e: e.tensor_tensor(
            qbf[sy][:, :], qab[sy][:, :], c1b[:, :], ALU.add),
            waits=[te, tqa] + sl_qbf[sy].take(), inc="vs")
        sl_qa[sy].users.append(tq)
        sl_c1.users.append(tq)
        return tq

    def ffn_block_seq(seq, src_of_tt, m_of, r_of, ri_ap, ri_tok, al_ap, al_tok,
                      brow_tok, post_fn):
        info = {}

        def stage_b_mm(tg):
            g = tg % 2
            slots4, tfm4 = info.pop(tg)
            ev_toks = [[] for _ in range(4)]
            for oh in range(2):
                mm4 = [None] * 4
                for km in range(KM):
                    sw = km % 2
                    twl = p.dma("sync", lambda e, km=km, oh=oh, sw=sw: e.dma_start(
                        out=wr[sw][:, 0:512],
                        in_=wscr["qwT"][km * 128:(km + 1) * 128, oh * 512:(oh + 1) * 512]),
                        waits=sl_wr[sw].take() + [wscr_tok["qwT"][km]], inc=f"wld{sw}")
                    for t4 in range(4):
                        b = 4 + t4
                        bw = sl_bank[b].take() if km == 0 else []
                        mm4[t4] = p.op("tensor", lambda e, km=km, sw=sw, g=g, t4=t4, b=b: e.matmul(
                            bank[b][:, :], qfm[g][t4][:, km * 128:(km + 1) * 128],
                            wr[sw][:, 0:512],
                            start=(km == 0), stop=False),
                            waits=[twl, tfm4[t4]] + bw, inc="ts")
                        sl_wr[sw].users.append(mm4[t4])
                        sl_qfm[g][t4].users.append(mm4[t4])
                for t4 in range(4):
                    b = 4 + t4
                    mm4[t4] = p.op("tensor", lambda e, oh=oh, b=b: e.matmul(
                        bank[b][:, :], onesb[:, :], brow[0:1, oh * 512:(oh + 1) * 512],
                        start=False, stop=True),
                        waits=[mm4[t4], brow_tok, t_onesb], inc="ts")
                for t4 in range(4):
                    b = 4 + t4
                    so = t4
                    tw_xo = sl_xo[so].take() if oh == 0 else []
                    tev = p.op("vector", lambda e, b=b, oh=oh, slot=slots4[t4], so=so, al_ap=al_ap: e.scalar_tensor_tensor(
                        xo[so][:, oh * 512:(oh + 1) * 512], bank[b][:, :], al_ap,
                        xr[slot][:, oh * 512:(oh + 1) * 512], ALU.mult, ALU.add),
                        waits=[mm4[t4], al_tok] + tw_xo, inc="vs")
                    sl_bank[b].users.append(tev)
                    sl_xr[slots4[t4]].users.append(tev)
                    ev_toks[t4].append(tev)
            return slots4, ev_toks

        for ttg in range(NTG + 1):
            pend_b = None
            if ttg >= 1:
                pend_b = stage_b_mm(ttg - 1)
            if ttg < NTG:
                g = ttg % 2
                slots4, tfm4 = [], []
                for t4 in range(4):
                    tt = ttg * 4 + t4
                    idx = seq * NT + tt
                    slot = idx % NXR
                    src, srcw = src_of_tt(tt)
                    tl = p.dma("sync", lambda e, src=src, slot=slot: e.dma_start(
                        out=xr[slot][:, :], in_=src),
                        waits=sl_xr[slot].take() + ([srcw] if srcw else []),
                        inc=f"xin{slot}")
                    tq = quant_chain(lambda slot=slot: xr[slot][:, :],
                                     m_of(idx), r_of(idx), ri_ap,
                                     sl_xr[slot].users, [tl, ri_tok], idx % 2)
                    tfm, _ = transpose_tile(qbf[idx % 2][:, :], qfm[g][t4][:, :],
                                            [tq] + sl_qfm[g][t4].take(), bank_i=t4)
                    sl_qbf[idx % 2].users.append(tfm)
                    slots4.append(slot)
                    tfm4.append(tfm)
                info[ttg] = (slots4, tfm4)
            if pend_b is not None:
                post_fn(ttg - 1, pend_b[0], pend_b[1])

    t_id32 = p.dma("sync", lambda e: e.dma_start(out=id32[:, :], in_=id32_in[:, :]), inc="ld")
    t_idbf = p.dma("sync", lambda e: e.dma_start(out=idbf[:, :], in_=idbf_in[:, :]), inc="ld")
    t_cos = p.dma("sync", lambda e: e.dma_start(out=costab[:, :], in_=cos_in[:, :]), inc="ld")
    t_sin = p.dma("sync", lambda e: e.dma_start(out=sintab[:, :], in_=sin_in[:, :]), inc="ld")
    t_mask = p.dma("sync", lambda e: e.dma_start(out=masks[:, :], in_=mask_in[:, :]), inc="ld")
    t_wob0 = p.dma("sync", lambda e: e.dma_start(out=wstage[0][0:1, :], in_=wob_in[:, :]), inc="wst0")
    t_ones = p.dma("sync", lambda e: e.dma_start(out=onesf[:, :], in_=onesf_in[:, :]), inc="ld")
    t_onesb = p.dma("sync", lambda e: e.dma_start(out=onesb[:, :], in_=onesb_in[:, :]), inc="ld")
    t_ccl = p.dma("sync", lambda e: e.dma_start(out=ccol[:, :], in_=ccol_in[:, :]), inc="ld")
    t_crl = p.dma("sync", lambda e: e.dma_start(out=crow[:, :], in_=crow_in[:, :]), inc="ld")
    t_mz = p.dma("sync", lambda e: e.dma_start(out=maxacc[:, :], in_=ccol_in[:, 6:8], allow_slow_non_contiguous=True), inc="ld")
    t_csh = p.dma("sync", lambda e: e.dma_start(out=cshift[:, :], in_=ccol_in[:, 4:5], allow_slow_non_contiguous=True), inc="ld")
    t_chf = p.dma("sync", lambda e: e.dma_start(out=chalf[:, :], in_=ccol_in[:, 5:6], allow_slow_non_contiguous=True), inc="ld")
    t_cc0 = [t_ccl, t_crl]
    t_wob = p.op("vector", lambda e: e.tensor_copy(wob_row[:, :], wstage[0][0:1, :]),
                 waits=[t_wob0], inc="vs")
    sl_wst[0].users.append(t_wob)
    t_vones = []
    for vf_ in range(2):
        for t_ in range(NT):
            t_vones.append(p.dma("sync", lambda e, vf_=vf_, t_=t_: e.dma_start(
                out=vpair[vf_][:, t_ * 192 + 64:t_ * 192 + 128], in_=vtones_in[:, :]),
                inc="ld"))
    t_vones = t_vones[-1]

    for wname, wap in [("wq", wq_in), ("wk", wk_in), ("wv", wv_in), ("wo", wo_in)]:
        for km in range(KM):
            s = km % 2
            tl = p.dma("sync", lambda e, wap=wap, km=km, s=s: e.dma_start(
                out=wstage[s][:, :], in_=wap[km * 128:(km + 1) * 128, :]),
                waits=sl_wst[s].take(), inc=f"wst{s}")
            tc = p.op("vector", lambda e, s=s: e.tensor_copy(wcast[:, :], wstage[s][:, :]),
                      waits=[tl] + sl_wcast.take(), inc="vs")
            sl_wst[s].users.append(tc)
            td = p.dma("gpsimd", lambda e, wname=wname, km=km: e.dma_start(
                out=wscr[wname][km * 128:(km + 1) * 128, :], in_=wcast[:, :]),
                waits=[tc], inc="swr")
            sl_wcast.users.append(td)
            wscr_tok[wname][km] = td

    prev_add = None
    for km in range(KM):
        s = km % 2
        tl = p.dma("sync", lambda e, km=km, s=s: e.dma_start(
            out=wstage[s][:, :], in_=fw_in[km * 128:(km + 1) * 128, :]),
            waits=sl_wst[s].take(), inc=f"wst{s}")
        col = 6 if km == 0 else 7
        ta = p.op("scalar", lambda e, s=s, col=col: e.activation(
            absw[:, :], wstage[s][:, :], AF.Abs, accum_out=stt[:, col:col + 1]),
            waits=[tl] + ([prev_add] if km > 0 else []), inc="ss")
        sl_wst[s].users.append(ta)
        if km == 0:
            prev_add = p.op("scalar", lambda e: e.drain(), [ta], inc="ss")
        else:
            tadr = p.op("scalar", lambda e: e.drain(), [ta], inc="ss")
            prev_add = vsm(lambda e: e.tensor_tensor(
                stt[:, 6:7], stt[:, 6:7], stt[:, 7:8], ALU.add), [tadr])
    t_ws_tr = p.op("tensor", lambda e: e.transpose(bank[7][0:1, 0:128], stt[:, 6:7], id32[:, :]),
                   waits=[prev_add, t_id32] + sl_bank[7].take(), inc="ts")
    t_ws_cp = vsm(lambda e: e.tensor_copy(mx128[:, :], bank[7][0:1, 0:128]), [t_ws_tr])
    sl_bank[7].users.append(t_ws_cp)
    t_ws = vsm(lambda e: e.tensor_reduce(scal[0:1, 0:1], mx128[:, :], AX.X, ALU.add), [t_ws_cp])
    t_ws2 = vsm(lambda e: e.tensor_tensor(
        scal[0:1, 0:1], scal[0:1, 0:1], crow[0:1, 1:2], ALU.mult), [t_ws] + t_cc0)
    t_thr = vsm(lambda e: e.tensor_tensor(
        scal[0:1, 1:2], scal[0:1, 0:1], crow[0:1, 2:3], ALU.mult), [t_ws2])
    t_thrb = bcast_col(scal[0:1, 1:2], thrcol[:, 0:1], [t_thr])

    for km in range(KM):
        s = km % 2
        tl = p.dma("sync", lambda e, km=km, s=s: e.dma_start(
            out=wstage[s][:, :], in_=fw_in[km * 128:(km + 1) * 128, :]),
            waits=sl_wst[s].take(), inc=f"wst{s}")
        tabs = p.op("scalar", lambda e, s=s: e.activation(absw[:, :], wstage[s][:, :], AF.Abs),
                    waits=[tl] + sl_absw.take(), inc="ss")
        tsg = p.op("scalar", lambda e, s=s: e.activation(sgnw[:, :], wstage[s][:, :], AF.Sign),
                   waits=[tl] + sl_u2.take(), inc="ss")
        sl_wst[s].users += [tabs, tsg]
        tgt = p.op("vector", lambda e: e.tensor_scalar(
            gtw[:, :], absw[:, :], thrcol[:, 0:1], None, ALU.is_gt),
            waits=[tabs, t_thrb] + sl_u1.take(), inc="vs")
        sl_absw.users.append(tgt)
        tqw = p.op("vector", lambda e: e.tensor_tensor(qwc[:, :], sgnw[:, :], gtw[:, :], ALU.mult),
                   waits=[tgt, tsg] + sl_qwc.take(), inc="vs")
        sl_u1.users.append(tqw)
        sl_u2.users.append(tqw)
        bw = sl_bank[0].take()
        last = None
        for j in range(KM):
            last = p.op("tensor", lambda e, j=j: e.transpose(
                bank[0].bitcast(BF16)[:, j * 128:(j + 1) * 128],
                qwc[:, j * 128:(j + 1) * 128], idbf[:, :]),
                waits=([tqw, t_idbf] + bw) if j == 0 else [], inc="ts")
        sl_qwc.users.append(last)
        tev = p.op("vector", lambda e: e.tensor_copy(qwt_ev[:, :], bank[0].bitcast(BF16)[:, 0:1024]),
                   waits=[last] + sl_qwtev.take(), inc="vs")
        sl_bank[0].users.append(tev)
        dst = wscr["qwT"].rearrange("(j r) c -> r j c", r=128)[:, :, km * 128:(km + 1) * 128]
        src = qwt_ev[:, :].rearrange("p (j c) -> p j c", j=KM)
        tqd = p.dma("gpsimd", lambda e, dst=dst, src=src: e.dma_start(out=dst, in_=src),
                    waits=[tev], inc="swr")
        sl_bank[0].users.append(tqd)
        sl_qwtev.users.append(tqd)
        wscr_tok["qwT"][km] = tqd
        sl_r1.users.append(last)
        sl_r2.users.append(sl_wcast.users[-1] if sl_wcast.users else None)

    sl_qa[0].users += sl_u2.take()
    sl_c1.users += sl_absw.take()
    sl_qbf[0].users += sl_qwtev.take()
    for i_ in range(2):
        sl_xo[i_].users += sl_wst[i_].take()

    for seq in range(SEQS):
        for ttg in range(NTG):
            tls, slots = [], []
            for t4 in range(4):
                tt = ttg * 4 + t4
                idx = seq * NT + tt
                slot = idx % NXR
                tl = p.dma("sync", lambda e, seq=seq, tt=tt, slot=slot: e.dma_start(
                    out=xr[slot][:, :], in_=x_in[seq, tt * 128:(tt + 1) * 128, :]),
                    waits=sl_xr[slot].take(), inc=f"xin{slot}")
                tls.append([tl])
                slots.append(slot)
            base = seq * NT + ttg * 4
            (stok, sq_toks) = ln_stats4(
                [xr[s][:, :] for s in slots],
                st1m[:, base:base + 4], st1r[:, base:base + 4], tls)
            zts = []
            for t4 in range(4):
                idx = base + t4
                slot = slots[t4]
                sy = t4 % 2
                tz = p.op("vector", lambda e, slot=slot, idx=idx, sy=sy: e.tensor_scalar(
                    yb[sy][:, :], xr[slot][:, :], st1m[:, idx:idx + 1], st1r[:, idx:idx + 1],
                    ALU.subtract, ALU.mult), waits=stok + sl_y[sy].take(), inc="vs")
                sl_xr[slot].users += [tz, sq_toks[t4]]
                tm = p.op("vector", lambda e, t4=t4, sy=sy: e.tensor_reduce(
                    stt[:, 16 + t4:17 + t4], yb[sy][:, :], AX.X, ALU.max,
                    apply_absolute_value=True), waits=[tz], inc="vs")
                sl_y[sy].users.append(tm)
                zts.append(tm)
            trd = p.op("vector", lambda e: e.drain(), zts, inc="vs")
            tmx = vsm(lambda e: e.tensor_reduce(
                stt[:, 20:21], stt[:, 16:20], AX.X, ALU.max), [trd])
            last_macc[0] = vsm(lambda e: e.tensor_tensor(
                maxacc[:, 0:1], maxacc[:, 0:1], stt[:, 20:21], ALU.max),
                [tmx, t_mz] + ([last_macc[0]] if last_macc[0] else []))

    def allreduce_scale(maxcol, cc_idx, ri_dst, al_dst):
        t_drain = p.op("vector", lambda e: e.drain(), waits=[p.cur("vs")], inc="vs")
        t_pm = part_reduce_max(maxcol, scal[0:1, 2:3], [t_drain])
        t_bc = vsm(lambda e: e.tensor_copy(
            scal[0:1, 3:16], scal[0:1, 2:3].broadcast_to((1, 13))), [t_pm])
        t_ccw = p.dma("gpsimd", lambda e: e.dma_start(
            out=cci[cc_idx].ap()[:, :], in_=scal[0:1, 0:16]), waits=[t_bc, t_ws2], inc="swr")
        t_cc = p.op("gpsimd", lambda e: e.collective_compute(
            "AllReduce", ALU.max, replica_groups=[list(range(NCORES))],
            ins=[cci[cc_idx].ap().opt()], outs=[cco[cc_idx].ap().opt()]),
            waits=[t_ccw], inc="cc", dma=True)
        t_ccr = p.dma("gpsimd", lambda e: e.dma_start(
            out=ccres[0:1, cc_idx * 16:(cc_idx + 1) * 16], in_=cco[cc_idx].ap()[0:1, 0:16]),
            waits=[t_cc], inc=f"gld{cc_idx}")
        t_is = vsm(lambda e: e.tensor_tensor(
            scal[0:1, 4:5], ccres[0:1, cc_idx * 16 + 2:cc_idx * 16 + 3],
            crow[0:1, 3:4], ALU.mult),
            [t_ccr] + t_cc0)
        t_ri = newton_recip(scal[0:1, 4:5], scal[0:1, 5:6], [t_is])
        t_al = vsm(lambda e: e.tensor_tensor(
            scal[0:1, 6:7], scal[0:1, 0:1], scal[0:1, 4:5], ALU.mult), [t_is])
        t_alb = vsm(lambda e: e.tensor_tensor(
            scal[0:1, 6:7], scal[0:1, 6:7], crow[0:1, 2:3], ALU.mult), [t_al])
        t_wsis = vsm(lambda e: e.tensor_tensor(
            scal[0:1, 7:8], scal[0:1, 0:1], scal[0:1, 4:5], ALU.mult), [t_is])
        t_rb = newton_recip(scal[0:1, 7:8], scal[0:1, 8:9], [t_wsis])
        t_fbl = p.dma("gpsimd", lambda e: e.dma_start(
            out=wcast[0:1, :], in_=fb_in[:, :]), waits=sl_wcast.take(), inc=f"gfb{cc_idx}")
        t_brow_ = p.op("vector", lambda e: e.tensor_scalar(
            brow[:, :], wcast[0:1, :], scal[0:1, 8:9], None, ALU.mult),
            waits=[t_rb, t_fbl], inc="vs")
        t_brow = p.op("vector", lambda e: e.drain(), [t_brow_], inc="vs")
        sl_wcast.users.append(t_brow)
        t_ric = bcast_col(scal[0:1, 5:6], ri_dst, [t_ri])
        t_alc = bcast_col(scal[0:1, 6:7], al_dst, [t_alb])
        return t_ric, t_alc, t_brow

    t_ri1c, t_al1c, t_brow1 = allreduce_scale(maxacc[:, 0:1], 0, ricol[:, 0:1], alcol[:, 0:1])

    for seq in range(SEQS):
        z2toks = []

        def ffn1_post(tg, slots4, ev_toks, seq=seq):
            base = seq * NT + tg * 4
            stok, sq_toks = ln_stats4([xo[t4][:, :] for t4 in range(4)],
                                      l2m[:, 0:4], l2r[:, 0:4],
                                      [ev_toks[t4] for t4 in range(4)])
            for t4 in range(4):
                tt = tg * 4 + t4
                tz = p.op("vector", lambda e, t4=t4: e.tensor_scalar(
                    z2tm[:, :], xo[t4][:, :], l2m[:, t4:t4 + 1], l2r[:, t4:t4 + 1],
                    ALU.subtract, ALU.mult),
                    waits=stok + [sq_toks[t4]] + sl_z2tm.take(), inc="vs")
                dstz = z2[:, :].rearrange("p (k t) -> p k t", k=KM)[:, :, tt * 128:(tt + 1) * 128]
                tfm, tlast_tr = transpose_tile(z2tm[:, :], dstz,
                                               [tz] + (sl_z2.take() if tt == 0 else []),
                                               strided_dst=True, bank_i=t4)
                sl_z2tm.users += [tfm, tlast_tr]
                td = p.dma("gpsimd", lambda e, seq=seq, tt=tt, t4=t4: e.dma_start(
                    out=xp_scr[seq * S + tt * 128: seq * S + (tt + 1) * 128, :],
                    in_=xo[t4][:, :]), waits=ev_toks[t4] + [tz], inc=f"sxp{seq}")
                sl_xo[t4].users += [tz, sq_toks[t4], td]
                xp_w[(seq, tt)] = td
                z2toks.append(tfm)

        ffn_block_seq(seq,
                      lambda tt, seq=seq: (x_in[seq, tt * 128:(tt + 1) * 128, :], None),
                      lambda idx: st1m[:, idx:idx + 1], lambda idx: st1r[:, idx:idx + 1],
                      ricol[:, 0:1], t_ri1c, alcol[:, 0:1], t_al1c, t_brow1, ffn1_post)

        attn_toks = []
        avpar = 0
        ptc = 0
        attn_first = True
        for hp in range(KM):
            swk = 0
            srck = wscr["wk"].rearrange("(k r) c -> r k c", r=128)[:, :, hp * 128:(hp + 1) * 128]
            dstk = wr[swk][:, :].rearrange("p (k c) -> p k c", k=KM)
            twlk = p.dma("sync", lambda e, srck=srck, dstk=dstk: e.dma_start(
                out=dstk, in_=srck),
                waits=sl_wr[swk].take() + [wscr_tok["wk"][k] for k in range(KM)],
                inc=f"wld{swk}")
            mmk = [None] * NQB
            for km in range(KM):
                for t512 in range(NQB):
                    b = t512
                    bw = sl_bank[b].take() if km == 0 else []
                    mmk[b] = p.op("tensor", lambda e, km=km, swk=swk, t512=t512, b=b: e.matmul(
                        bank[b][:, :], wr[swk][:, km * 128:(km + 1) * 128],
                        z2[:, km * S + t512 * 512: km * S + (t512 + 1) * 512],
                        start=(km == 0), stop=(km == KM - 1)),
                        waits=[twlk] + z2toks + bw, inc="ts")
                    sl_wr[swk].users.append(mmk[b])
                    sl_z2.users.append(mmk[b])
            kev = []
            ktw = sl_ktmp.take()
            for t512 in range(NQB):
                te = p.op("vector", lambda e, t512=t512: e.tensor_copy(
                    ktmp[:, t512 * 512:(t512 + 1) * 512], bank[t512][:, :]),
                    waits=[mmk[t512]] + (ktw if t512 == 0 else []), inc="vs")
                sl_bank[t512].users.append(te)
                kev.append(te)
            kf = hp % 2
            kfw = sl_kfm[kf].take()
            tkf = None
            for sl_ in range(NQB):
                lo = sl_ * 512
                tk1 = p.op("gpsimd", lambda e, lo=lo: e.tensor_tensor(
                    kropet[:, :], ktmp[:, lo:lo + 512], costab[:, lo:lo + 512], ALU.mult),
                    waits=kev + [t_cos]
                    + ([tkf] if tkf else ([krope_prev[0]] if krope_prev[0] else []))
                    + (kfw if sl_ == 0 else []), inc="gs")
                tks = []
                for g in range(4):
                    gs = [1, 0, 3, 2][g]
                    tks.append(p.op("gpsimd", lambda e, g=g, gs=gs, lo=lo: e.tensor_tensor(
                        kropes[g * 32:(g + 1) * 32, :],
                        ktmp[gs * 32:(gs + 1) * 32, lo:lo + 512],
                        sintab[gs * 32:(gs + 1) * 32, lo:lo + 512], ALU.mult),
                        waits=[tk1, t_sin], inc="gs"))
                tkf = p.op("gpsimd", lambda e, kf=kf, lo=lo: e.tensor_tensor(
                    kfm[kf][:, lo:lo + 512], kropet[:, :], kropes[:, :],
                    ALU.add), waits=tks, inc="gs")
                sl_ktmp.users.append(tkf)
            krope_prev[0] = tkf
            krope_tok = tkf

            swv = 1
            srcv = wscr["wv"].rearrange("(k r) c -> r k c", r=128)[:, :, hp * 128:(hp + 1) * 128]
            dstv = wr[swv][:, :].rearrange("p (k c) -> p k c", k=KM)
            twlv = p.dma("sync", lambda e, srcv=srcv, dstv=dstv: e.dma_start(
                out=dstv, in_=srcv),
                waits=sl_wr[swv].take() + [wscr_tok["wv"][k] for k in range(KM)],
                inc=f"wld{swv}")
            vf = hp % 2
            vfw = sl_vp[vf].take()
            for t in range(NT):
                b = 4 + (t % 2)
                mmv = None
                bw = sl_bank[b].take()
                for km in range(KM):
                    mmv = p.op("tensor", lambda e, km=km, swv=swv, t=t, b=b: e.matmul(
                        bank[b][:, 0:128], z2[:, km * S + t * 128: km * S + (t + 1) * 128],
                        wr[swv][:, km * 128:(km + 1) * 128],
                        start=(km == 0), stop=(km == KM - 1)),
                        waits=[twlv] + z2toks + (bw if km == 0 else []), inc="ts")
                    sl_wr[swv].users.append(mmv)
                    sl_z2.users.append(mmv)
                dst_v = vpair[vf][:, :].rearrange("p (t blk c) -> p t blk c", t=NT, blk=3)[:, t, 0:3:2, :]
                tev = p.op("vector", lambda e, t=t, b=b, vf=vf, dst_v=dst_v: e.tensor_copy(
                    dst_v, bank[b][:, 0:128]),
                    waits=[mmv] + (vfw if t == 0 else []), inc="vs")
                sl_bank[b].users.append(tev)
                sl_vp[vf].users.append(tev)
                if t == 0:
                    v_toks = []
                v_toks.append(tev)

            sw = 0
            srcw_ap = wscr["wq"].rearrange("(k r) c -> r k c", r=128)[:, :, hp * 128:(hp + 1) * 128]
            dstw_ap = wr[sw][:, :].rearrange("p (k c) -> p k c", k=KM)
            twl = p.dma("sync", lambda e, srcw_ap=srcw_ap, dstw_ap=dstw_ap: e.dma_start(
                out=dstw_ap, in_=srcw_ap),
                waits=sl_wr[sw].take() + [wscr_tok["wq"][k] for k in range(KM)], inc=f"wld{sw}")
            mm_last = [None] * NQB
            for km in range(KM):
                for t512 in range(NQB):
                    b = t512
                    bw = sl_bank[b].take() if km == 0 else []
                    mm_last[b] = p.op("tensor", lambda e, km=km, sw=sw, t512=t512, b=b: e.matmul(
                        bank[b][:, :], wr[sw][:, km * 128:(km + 1) * 128],
                        z2[:, km * S + t512 * 512: km * S + (t512 + 1) * 512],
                        start=(km == 0), stop=(km == KM - 1)),
                        waits=[twl] + z2toks + bw, inc="ts")
                    sl_wr[sw].users.append(mm_last[b])
                    sl_z2.users.append(mm_last[b])
            shp = hp % 2
            tqf = None
            qhw = sl_qhp[shp].take()
            for sl_ in range(NQB):
                lo = sl_ * 512
                tq1 = p.op("vector", lambda e, sl_=sl_, lo=lo: e.tensor_tensor(
                    ropet[:, :], bank[sl_][:, :], costab[:, lo:lo + 512], ALU.mult),
                    waits=[mm_last[sl_], t_cos]
                    + ([tqf] if tqf else ([prev_tqf[0]] if prev_tqf[0] else [])), inc="vs")
                tqs = []
                for g in range(4):
                    gsw = [1, 0, 3, 2][g]
                    tqs.append(p.op("vector", lambda e, sl_=sl_, g=g, gsw=gsw, lo=lo: e.tensor_tensor(
                        ropes[g * 32:(g + 1) * 32, :], bank[sl_][gsw * 32:(gsw + 1) * 32, :],
                        sintab[gsw * 32:(gsw + 1) * 32, lo:lo + 512], ALU.mult),
                        waits=[tq1, t_sin], inc="vs"))
                tqf = p.op("vector", lambda e, shp=shp, lo=lo: e.tensor_tensor(
                    qhp[shp][:, lo:lo + 512], ropet[:, :], ropes[:, :], ALU.add),
                    waits=tqs + (qhw if sl_ == 0 else []), inc="vs")
                sl_bank[sl_].users += [tqf]
            prev_tqf[0] = tqf

            for h2 in range(2):
                head = 2 * hp + h2
                for qb in range(NQB):
                    nkc = (qb + 1) * 4
                    avb = 6 if avpar == 0 else 7
                    avw = sl_bank[avb].take()
                    av_mm = None
                    pend = []
                    for kc in range(nkc + 1):
                        if kc < nkc:
                            sb_ = 4 + (kc % 2)
                            sw_b = sl_bank[sb_].take()
                            tmm = p.op("tensor", lambda e, h2=h2, kc=kc, qb=qb, sb_=sb_, shp=shp, kf=kf: e.matmul(
                                bank[sb_][:, :],
                                kfm[kf][h2 * 64:(h2 + 1) * 64, kc * 128:(kc + 1) * 128],
                                qhp[shp][h2 * 64:(h2 + 1) * 64, qb * 512:(qb + 1) * 512],
                                start=True, stop=True),
                                waits=[tqf, krope_tok] + sw_b, inc="ts")
                            sl_qhp[shp].users.append(tmm)
                            sl_kfm[kf].users.append(tmm)
                            pt = ptc % 3
                            ptc += 1
                            texp = p.op("scalar", lambda e, sb_=sb_, pt=pt: e.activation(
                                ptile[pt][:, :], bank[sb_][:, :], AF.Exp,
                                bias=cshift[:, 0:1], scale=0.125),
                                waits=[tmm, t_csh] + sl_pt[pt].take(), inc="ss")
                            sl_bank[sb_].users.append(texp)
                            tlast = texp
                            j = kc - 4 * qb
                            if 0 <= j < 4:
                                tlast = p.op("vector", lambda e, pt=pt, j=j: e.tensor_tensor(
                                    ptile[pt][:, :], ptile[pt][:, :],
                                    masks[:, j * 512:(j + 1) * 512], ALU.mult),
                                    waits=[texp, t_mask], inc="vs")
                            pend.append((kc, pt, tlast))
                        if (kc >= 1 or kc == nkc) and pend:
                            kca, pt, tlast = pend.pop(0)
                            av_mm = p.op("tensor", lambda e, kc=kca, h2=h2, pt=pt, avb=avb, nkc=nkc, vf=vf: e.matmul(
                                bank[avb][:, :],
                                vpair[vf][:, kc * 192 + h2 * 64: kc * 192 + h2 * 64 + 128],
                                ptile[pt][:, :],
                                start=(kc == 0), stop=(kc == nkc - 1)),
                                waits=[tlast, v_toks[kca], t_vones]
                                + (avw if kca == 0 else []), inc="ts")
                            sl_vp[vf].users.append(av_mm)
                            sl_pt[pt].users.append(av_mm)
                    while pend:
                        raise RuntimeError("pipeline leftover")
                    rp = avpar
                    zlo = 64 if h2 == 0 else 0
                    alo = 0 if h2 == 0 else 64
                    trc = p.op("vector", lambda e, avb=avb, rp=rp, zlo=zlo: e.reciprocal(
                        recb[rp], bank[avb][zlo:zlo + 64, :]),
                        waits=[av_mm] + sl_rec[rp].take(), inc="vs")
                    tat = p.op("vector", lambda e, avb=avb, rp=rp, h2=h2, hp=hp, qb=qb, alo=alo: e.tensor_tensor(
                        attn[h2 * 64:(h2 + 1) * 64, hp * S + qb * 512: hp * S + (qb + 1) * 512],
                        bank[avb][alo:alo + 64, :], recb[rp], ALU.mult),
                        waits=[trc] + (sl_attn.take() if attn_first else []), inc="vs")
                    attn_first = False
                    sl_bank[avb].users += [trc, tat]
                    sl_rec[rp].users.append(tat)
                    attn_toks.append(tat)
                    avpar = 1 - avpar

        for ttg in range(NTG):
            xp_toks = []
            for t4 in range(4):
                tt = ttg * 4 + t4
                slot = tt % NXR
                tl = p.dma("sync", lambda e, seq=seq, tt=tt, slot=slot: e.dma_start(
                    out=xr[slot][:, :],
                    in_=xp_scr[seq * S + tt * 128: seq * S + (tt + 1) * 128, :]),
                    waits=sl_xr[slot].take() + [xp_w[(seq, tt)]], inc=f"xin{slot}")
                xp_toks.append((slot, tl))
            mm_last = [None] * 8
            for kf in range(KM):
                sw = kf % 2
                twl = p.dma("sync", lambda e, kf=kf, sw=sw: e.dma_start(
                    out=wr[sw][:, :], in_=wscr["wo"][kf * 128:(kf + 1) * 128, :]),
                    waits=sl_wr[sw].take() + [wscr_tok["wo"][kf]], inc=f"wld{sw}")
                for t4 in range(4):
                    for oh in range(2):
                        b = t4 * 2 + oh
                        tt = ttg * 4 + t4
                        bw = sl_bank[b].take() if kf == 0 else []
                        mm = p.op("tensor", lambda e, kf=kf, sw=sw, tt=tt, oh=oh, b=b: e.matmul(
                            bank[b][:, :], attn[:, kf * S + tt * 128: kf * S + (tt + 1) * 128],
                            wr[sw][:, oh * 512:(oh + 1) * 512],
                            start=(kf == 0), stop=False),
                            waits=[twl] + attn_toks + bw, inc="ts")
                        mm_last[b] = mm
                        sl_wr[sw].users.append(mm)
                        sl_attn.users.append(mm)
            for t4 in range(4):
                for oh in range(2):
                    b = t4 * 2 + oh
                    mm_last[b] = p.op("tensor", lambda e, oh=oh, b=b: e.matmul(
                        bank[b][:, :], onesb[:, :], wob_row[0:1, oh * 512:(oh + 1) * 512],
                        start=False, stop=True),
                        waits=[mm_last[b], t_wob, t_onesb], inc="ts")
            ev_toks = [[] for _ in range(4)]
            for t4 in range(4):
                tt = ttg * 4 + t4
                slot, tlx = xp_toks[t4]
                so = t4
                tw_xo = sl_xo[so].take()
                for oh in range(2):
                    b = t4 * 2 + oh
                    tev = p.op("vector", lambda e, b=b, oh=oh, slot=slot, so=so: e.tensor_tensor(
                        xo[so][:, oh * 512:(oh + 1) * 512], bank[b][:, :],
                        xr[slot][:, oh * 512:(oh + 1) * 512], ALU.add),
                        waits=[mm_last[b], tlx] + (tw_xo if oh == 0 else []), inc="vs")
                    sl_bank[b].users.append(tev)
                    ev_toks[t4].append(tev)
                sl_xr[slot].users += ev_toks[t4]
            base = seq * NT + ttg * 4
            stok, sq_toks = ln_stats4([xo[t4][:, :] for t4 in range(4)],
                                      st3m[:, base:base + 4], st3r[:, base:base + 4],
                                      [ev_toks[t4] for t4 in range(4)])
            zts = []
            for t4 in range(4):
                tt = ttg * 4 + t4
                idx = base + t4
                sy = t4 % 2
                tz = p.op("vector", lambda e, t4=t4, idx=idx, sy=sy: e.tensor_scalar(
                    yb[sy][:, :], xo[t4][:, :], st3m[:, idx:idx + 1], st3r[:, idx:idx + 1],
                    ALU.subtract, ALU.mult), waits=stok + sl_y[sy].take(), inc="vs")
                tm = p.op("vector", lambda e, t4=t4, sy=sy: e.tensor_reduce(
                    stt[:, 16 + t4:17 + t4], yb[sy][:, :], AX.X, ALU.max,
                    apply_absolute_value=True), waits=[tz], inc="vs")
                sl_y[sy].users.append(tm)
                zts.append(tm)
                td = p.dma("gpsimd", lambda e, seq=seq, tt=tt, t4=t4: e.dma_start(
                    out=xpp_scr[seq * S + tt * 128: seq * S + (tt + 1) * 128, :],
                    in_=xo[t4][:, :]), waits=ev_toks[t4] + [tz], inc=f"sxpp{seq}")
                sl_xo[t4].users += [tz, sq_toks[t4], td]
                xpp_w[(seq, tt)] = td
            trd = p.op("vector", lambda e: e.drain(), zts, inc="vs")
            tmx = vsm(lambda e: e.tensor_reduce(
                stt[:, 20:21], stt[:, 16:20], AX.X, ALU.max), [trd])
            last_macc[1] = vsm(lambda e: e.tensor_tensor(
                maxacc[:, 1:2], maxacc[:, 1:2], stt[:, 20:21], ALU.max),
                [tmx] + ([last_macc[1]] if last_macc[1] else []))

    t_ri2c, t_al2c, t_brow2 = allreduce_scale(maxacc[:, 1:2], 1, ricol[:, 1:2], alcol[:, 1:2])

    for seq in range(SEQS):
        def ffn2_post(tg, slots4, ev_toks, seq=seq):
            for t4 in range(4):
                tt = tg * 4 + t4
                td = p.dma("gpsimd", lambda e, seq=seq, tt=tt, t4=t4: e.dma_start(
                    out=out[seq, tt * 128:(tt + 1) * 128, :], in_=xo[t4][:, :]),
                    waits=ev_toks[t4], inc="sout")
                sl_xo[t4].users.append(td)

        ffn_block_seq(seq,
                      lambda tt, seq=seq: (
                          xpp_scr[seq * S + tt * 128: seq * S + (tt + 1) * 128, :],
                          xpp_w[(seq, tt)]),
                      lambda idx: st3m[:, idx:idx + 1], lambda idx: st3r[:, idx:idx + 1],
                      ricol[:, 1:2], t_ri2c, alcol[:, 1:2], t_al2c, t_brow2, ffn2_post)

    if DBG:
        dbg = {}
        for nm, sh in [("dbg_xp", [SEQS * S, D]), ("dbg_xpp", [SEQS * S, D])]:
            dbg[nm] = nc.dram_tensor(nm, sh, F32, kind="ExternalOutput").ap()
        STW = [p.cur(s_) for s_ in list(p.counts)
               if s_.startswith(("sout", "sxp", "swr", "gld", "gfb"))]
        for nm, t, sh in [("dbg_scal", scal, [1, 16]), ("dbg_ccres", ccres, [1, 32]),
                          ("dbg_st1m", st1m, [128, SEQS * NT]),
                          ("dbg_st1r", st1r, [128, SEQS * NT]),
                          ("dbg_st3m", st3m, [128, SEQS * NT]),
                          ("dbg_st3r", st3r, [128, SEQS * NT]),
                          ("dbg_ricol", ricol, [128, 2]), ("dbg_alcol", alcol, [128, 2]),
                          ("dbg_maxacc", maxacc, [128, 2])]:
            a = nc.dram_tensor(nm, sh, F32, kind="ExternalOutput").ap()
            p.dma("gpsimd", lambda e, a=a, t=t: e.dma_start(out=a[:, :], in_=t[:, :]),
                  waits=STW + [p.cur("vs"), p.cur("gs"), p.cur("ts")], inc="swr")
        for nm, t, sh in [("dbg_z2", z2, [128, KM * S]),
                          ("dbg_attn", attn, [128, KM * S]),
                          ("dbg_q7", qhp[1], [128, S])]:
            a = nc.dram_tensor(nm, sh, BF16, kind="ExternalOutput").ap()
            p.dma("gpsimd", lambda e, a=a, t=t: e.dma_start(out=a[:, :], in_=t[:, :]),
                  waits=STW + [p.cur("vs"), p.cur("gs"), p.cur("ts")], inc="swr")
        qwt_dbg = nc.dram_tensor("dbg_qwt", [D, D], BF16, kind="ExternalOutput").ap()
        for i in range(KM):
            tl = p.dma("sync", lambda e, i=i: e.dma_start(
                out=wr[0][:, :], in_=wscr["qwT"][i * 128:(i + 1) * 128, :]),
                waits=STW + [p.cur("ts")] + sl_wr[0].take(), inc="wld0")
            td = p.dma("gpsimd", lambda e, i=i: e.dma_start(
                out=qwt_dbg[i * 128:(i + 1) * 128, :], in_=wr[0][:, :]),
                waits=[tl], inc="swr")
            sl_wr[0].users.append(td)
        for nm, srct in [("dbg_xp", xp_scr), ("dbg_xpp", xpp_scr)]:
            for i in range(SEQS * NT):
                tl = p.dma("sync", lambda e, srct=srct, i=i: e.dma_start(
                    out=xr[0][:, :], in_=srct[i * 128:(i + 1) * 128, :]),
                    waits=STW + [p.cur("vs")] + sl_xr[0].take(), inc="xin0")
                td = p.dma("gpsimd", lambda e, nm=nm, i=i: e.dma_start(
                    out=dbg[nm][i * 128:(i + 1) * 128, :], in_=xr[0][:, :]),
                    waits=[tl], inc="swr")
                sl_xr[0].users.append(td)

    p.op("gpsimd", lambda e: e.nop(), waits=[p.cur(s_) for s_ in list(p.counts) if s_.startswith(("sout", "sxp", "swr", "gld", "gfb"))])

    sems = {}
    for s_ in p.counts:
        sems[s_] = es.enter_context(nc.semaphore(s_))
    with nc.Block() as block:
        @block.sync
        def _(eng):
            p.emit("sync", eng, sems)

        @block.vector
        def _(eng):
            p.emit("vector", eng, sems)

        @block.scalar
        def _(eng):
            p.emit("scalar", eng, sems)

        @block.tensor
        def _(eng):
            p.emit("tensor", eng, sems)

        @block.gpsimd
        def _(eng):
            p.emit("gpsimd", eng, sems)

    es.close()
    return nc


def kernel(L0_attn_x, L0_attn_wq, L0_attn_wk, L0_attn_wv, L0_attn_wo_w,
           L0_attn_wo_b, L0_ffn_weight, L0_ffn_bias):
    B, S, D_ = L0_attn_x.shape
    assert D_ == D and B % NCORES == 0
    SEQS = B // NCORES
    x = np.ascontiguousarray(np.asarray(L0_attn_x, dtype=np.float32))
    perm = rope_perm()
    wq = np.ascontiguousarray(np.asarray(L0_attn_wq, np.float32)[:, perm])
    wk = np.ascontiguousarray(np.asarray(L0_attn_wk, np.float32)[:, perm])
    wv = np.ascontiguousarray(np.asarray(L0_attn_wv, np.float32))
    wo = np.ascontiguousarray(np.asarray(L0_attn_wo_w, np.float32))
    wob = np.ascontiguousarray(np.asarray(L0_attn_wo_b, np.float32)).reshape(1, D)
    fw = np.ascontiguousarray(np.asarray(L0_ffn_weight, np.float32))
    fb = np.ascontiguousarray(np.asarray(L0_ffn_bias, np.float32)).reshape(1, D)
    costab, sintab, masks = host_tables(S)
    id32 = np.eye(128, dtype=np.float32)
    idbf = np.eye(128, dtype=ml_dtypes.bfloat16)
    ccol_c = np.zeros((128, 32), np.float32)
    ccol_c[:, 16:20] = 1.0 / D
    ccol_c[:, 20:24] = EPS
    ccol_c[:, 24:28] = 3.0
    ccol_c[:, 28:32] = -0.5
    ccol_c[:, 0] = 1.0 / D
    ccol_c[:, 1] = EPS
    ccol_c[:, 2] = 3.0
    ccol_c[:, 3] = -0.5
    ccol_c[:, 4] = C_SHIFT
    ccol_c[:, 5] = -0.5
    crow_c = np.zeros((1, 16), np.float32)
    crow_c[0, :5] = [2.0, 1.0 / (D * D), 0.5, 1.0 / 127.0, -1.0]
    onesf_c = np.ones((1, 128), np.float32)
    onesb_c = np.ones((1, 128), ml_dtypes.bfloat16)
    vtones_c = np.ones((128, 64), ml_dtypes.bfloat16)

    nc = build_nc(SEQS, S)
    in_maps = []
    for c in range(NCORES):
        in_maps.append({
            "x": x[c * SEQS:(c + 1) * SEQS],
            "wq": wq, "wk": wk, "wv": wv, "wo": wo,
            "wob": wob, "fw": fw, "fb": fb,
            "id32": id32, "idbf": idbf,
            "costab": costab, "sintab": sintab, "masks": masks,
            "ccol_c": ccol_c, "crow_c": crow_c, "onesf_c": onesf_c,
            "onesb_c": onesb_c, "vtones_c": vtones_c,
        })
    res = run_bass_kernel_spmd(nc, in_maps, core_ids=list(range(NCORES)))
    if DBG:
        return res
    return np.concatenate([r["out"] for r in res.results], axis=0)
